# version 03
# baseline (speedup 1.0000x reference)
import numpy as np
import ml_dtypes
from contextlib import ExitStack

import concourse.bass as bass
import concourse.mybir as mybir
from concourse.bass_utils import run_bass_kernel_spmd

F32 = mybir.dt.float32
BF16 = mybir.dt.bfloat16
AF = mybir.ActivationFunctionType
ALU = mybir.AluOpType
AXX = mybir.AxisListType.X
BF = ml_dtypes.bfloat16

D = 4096
T = 1024
FC = 32
TC = 8
L = 128
NH = 128
PD = 64
NG = 8
NS = 128
DIN = 8192
CONVD = 10240
PROJ = 18560
EPS = 1e-5
GROUPS = [[0, 1, 2, 3], [4, 5, 6, 7]]

DBG = False



def _prep_lhsT(w_in_out):
    IN, OUT = w_in_out.shape
    assert IN % 128 == 0 and OUT % 128 == 0
    a = w_in_out.reshape(IN // 128, 128, OUT // 128, 128).transpose(2, 1, 0, 3)
    return np.ascontiguousarray(a.reshape(OUT, IN))


def stage_inputs(inp):
    s = {}
    f32 = np.float32
    s["wqkv_l"] = _prep_lhsT(np.asarray(inp["L0_attn_Wqkv"], f32))
    s["wv_r"] = np.ascontiguousarray(np.asarray(inp["L0_attn_Wqkv"], f32)[:, 2048:3072])
    s["bqkv_c"] = np.ascontiguousarray(np.asarray(inp["L0_attn_bqkv"], f32).reshape(24, 128).T)
    s["bv_row"] = np.ascontiguousarray(np.asarray(inp["L0_attn_bqkv"], f32)[2048:3072].reshape(1, 1024))
    s["wo_l"] = _prep_lhsT(np.asarray(inp["L0_attn_Wo"], f32))
    s["wb0_l"] = _prep_lhsT(np.ascontiguousarray(np.asarray(inp["L0_ffn_W_base"], f32).T))
    bsum = np.asarray(inp["L0_attn_bo"], f32) + np.asarray(inp["L0_ffn_b_base"], f32)
    s["b01_c"] = np.ascontiguousarray(bsum.reshape(32, 128).T)
    s["wg0_r"] = np.ascontiguousarray(np.asarray(inp["L0_ffn_W_gate"], f32).T)
    s["a0_l"] = _prep_lhsT(np.ascontiguousarray(
        np.asarray(inp["L0_ffn_A"], f32).reshape(128, 4096).T))
    s["b0f_l"] = _prep_lhsT(np.ascontiguousarray(
        np.asarray(inp["L0_ffn_B"], f32).transpose(0, 2, 1).reshape(128, 4096)))
    s["eexp1"] = np.ascontiguousarray(np.repeat(np.eye(8, dtype=f32), 16, axis=1))
    s["eexp2"] = np.ascontiguousarray(2.0 * s["eexp1"])
    s["nw_c"] = np.ascontiguousarray(np.asarray(inp["L1_attn_norm_w"], f32).reshape(32, 128).T)
    s["wip_l"] = _prep_lhsT(np.ascontiguousarray(np.asarray(inp["L1_attn_in_proj_w"], f32).T))
    s["convw_c"] = np.ascontiguousarray(
        np.asarray(inp["L1_attn_conv_w"], f32).reshape(80, 128, 4).transpose(1, 0, 2).reshape(128, 320))
    s["convb_c"] = np.ascontiguousarray(np.asarray(inp["L1_attn_conv_b"], f32).reshape(80, 128).T)
    s["dtb_c"] = np.ascontiguousarray(np.asarray(inp["L1_attn_dt_bias"], f32).reshape(128, 1))
    s["alog_c"] = np.ascontiguousarray(np.asarray(inp["L1_attn_A_log"], f32).reshape(128, 1))
    s["dp_row"] = np.ascontiguousarray(np.asarray(inp["L1_attn_Dp"], f32).reshape(1, 128))
    s["gnw_row"] = np.ascontiguousarray(np.asarray(inp["L1_attn_gnorm_w"], f32).reshape(1, 8192))
    wopT = np.ascontiguousarray(np.asarray(inp["L1_attn_out_proj_w"], f32).T)
    s["wop_k"] = np.ascontiguousarray(wopT.reshape(64, 128, 32, 128).transpose(0, 2, 1, 3))
    s["wb1_l"] = _prep_lhsT(np.ascontiguousarray(np.asarray(inp["L1_ffn_W_base"], f32).T))
    s["bb1_c"] = np.ascontiguousarray(np.asarray(inp["L1_ffn_b_base"], f32).reshape(32, 128).T)
    s["wg1_r"] = np.ascontiguousarray(np.concatenate(
        [np.asarray(inp["L1_ffn_W_gate"], f32).T, np.asarray(inp["L1_ffn_W_thr"], f32).T], axis=1))
    s["bthr"] = np.ascontiguousarray(np.asarray(inp["L1_ffn_b_thr"], f32).reshape(1, 1))
    s["a1_l"] = _prep_lhsT(np.ascontiguousarray(
        np.asarray(inp["L1_ffn_A"], f32).reshape(128, 4096).T))
    s["b1f_l"] = _prep_lhsT(np.ascontiguousarray(
        np.asarray(inp["L1_ffn_B"], f32).transpose(0, 2, 1).reshape(128, 4096)))

    x = np.asarray(inp["L0_attn_x"], f32)
    per = []
    for r in range(8):
        b, q = r // 4, r % 4
        d = {}
        d["x0T"] = np.ascontiguousarray(x[b, q * 1024:(q + 1) * 1024, :].T)
        d["halo_sel"] = np.full((128, 1), 0.0 if q == 0 else 1.0, f32)
        selT = np.zeros((4, 128), f32)
        mb = np.zeros((4, 1), f32)
        for m in range(4):
            mb[m, 0] = 1.0 if m < q else 0.0
            for j in range(4):
                if m < j < q:
                    selT[j, m] = 1.0
        d["selT"] = selT
        d["maskbit"] = mb
        per.append(d)
    return s, per



class Sem:
    def __init__(self, ctx, name):
        self.h = ctx.es.enter_context(ctx.nc.semaphore(name))
        self.v = 0

    def inc(self, instr, n=1):
        instr.then_inc(self.h, n)
        self.v += n
        return self.v

    def dinc(self, instr):
        instr.then_inc(self.h, 16)
        self.v += 16
        return self.v


class EChain:

    def __init__(self, eng, sem):
        self.eng = eng
        self.sem = sem

    def __call__(self, instr):
        self.sem.inc(instr)
        self.eng.wait_ge(self.sem.h, self.sem.v)
        return instr


class Ctx:
    def __init__(self, name, fused=False):
        self.nc = bass.Bass("TRN2", target_bir_lowering=False)
        self.es = ExitStack()
        self.dram = {}
        self.in_names = []
        self.out_names = []
        self.sems = {}
        self.uid = 0
        self.fused = fused
        self.prefix = ""
        self.gstack = None
        self.cm = None

    def begin_group(self, name):
        if self.fused:
            self.prefix = name + "_"
            self.gstack = ExitStack()

    def end_group(self):
        if self.fused:
            self.gstack.close()
            self.gstack = None
            self.prefix = ""

    def sem(self, name):
        if name not in self.sems:
            self.sems[name] = Sem(self, name)
        return self.sems[name]

    def din(self, name, shape, dtype=F32):
        if name not in self.dram:
            self.dram[name] = self.nc.dram_tensor(name, list(shape), dtype, kind="ExternalInput").ap()
            self.in_names.append(name)
        return self.dram[name]

    def dout(self, name, shape, dtype=F32):
        if self.fused and name != "outT":
            return self.dint(name, shape, dtype)
        if name not in self.dram:
            self.dram[name] = self.nc.dram_tensor(name, list(shape), dtype, kind="ExternalOutput").ap()
            self.out_names.append(name)
        return self.dram[name]

    def dint(self, name, shape, dtype=F32):
        if name not in self.dram:
            self.dram[name] = self.nc.dram_tensor(name, list(shape), dtype, kind="Internal").ap()
        return self.dram[name]

    def sb(self, name, shape, dtype):
        st = self.gstack if self.gstack is not None else self.es
        return st.enter_context(self.nc.sbuf_tensor(self.prefix + name, list(shape), dtype))

    def sbx(self, stack, name, shape, dtype):
        return stack.enter_context(self.nc.sbuf_tensor(self.prefix + name, list(shape), dtype, side="right"))

    def uname(self, base):
        self.uid += 1
        return f"{self.prefix}{base}_{self.uid}"

    def ps(self, name, shape, dtype=F32):
        st = self.gstack if self.gstack is not None else self.es
        return st.enter_context(self.nc.psum_tensor(self.prefix + name, list(shape), dtype))


def slab_ap(dram, row0, n_oc, IN):
    return bass.AP(dram.tensor, row0 * IN, [[IN, 128], [128 * IN, n_oc], [1, IN]])


def colbc(tile, col0, dims):
    base = tile[:, col0:col0 + 1]
    full = tile[:, :]
    return bass.AP(tile, base.offset, [list(full.ap[0])] + dims)


def setup_common(ctx):
    if ctx.cm is not None:
        ctx.cm_eps1 = ctx.cm["epsc"][0:1, 0:1]
        return ctx.cm
    nc = ctx.nc
    c = {}

    def _p(n, sh, dt):
        return ctx.es.enter_context(nc.sbuf_tensor(n, sh, dt))
    c["iot"] = _p("c_iot", [128, 128], F32)
    c["pidx"] = _p("c_pidx", [128, 1], F32)
    c["identb"] = _p("c_identb", [128, 128], BF16)
    c["ones_bf"] = _p("c_onesbf", [128, 128], BF16)
    c["onescol"] = _p("c_onescol", [1, 128], BF16)
    c["epsc"] = _p("c_epsc", [1, 1], F32)
    c["eps128"] = _p("c_eps128", [128, 1], F32)
    with nc.Block() as b:
        @b.gpsimd
        def _(gp):
            gp.iota(c["iot"][:, :], [[1, 128]], channel_multiplier=0, allow_small_or_imprecise_dtypes=True)
            gp.iota(c["pidx"][:, :], [[1, 1]], channel_multiplier=1, allow_small_or_imprecise_dtypes=True)
            gp.memset(c["ones_bf"][:, :], 1.0)
            gp.memset(c["onescol"][:, :], 1.0)
            gp.memset(c["epsc"][:, :], EPS)
            gp.memset(c["eps128"][:, :], EPS)
    with nc.Block() as b:
        @b.vector
        def _(v):
            v.tensor_scalar(c["identb"][:, :], c["iot"][:, :], c["pidx"][:, 0:1], None, ALU.is_equal)
    ctx.cm = c
    ctx.cm_eps1 = c["epsc"][0:1, 0:1]
    return c


def ln_rows(ctx, ps_s, ps_q, pre, want_rms=False, stack=None):
    nc = ctx.nc
    r = {}
    al = (lambda n, sh, dt: ctx.sbx(stack, n, sh, dt)) if stack is not None else ctx.sb
    m = al(pre + "m", [1, 1024], F32)
    ms = al(pre + "ms", [1, 1024], F32)
    var = al(pre + "var", [1, 1024], F32)
    rstd = al(pre + "rstd", [1, 1024], F32)
    nmr = al(pre + "nmr", [1, 1024], F32)
    r["rstd_b"] = ctx.sb(pre + "rstdb", [1, 1024], BF16)
    r["nmr_b"] = ctx.sb(pre + "nmrb", [1, 1024], BF16)
    if want_rms:
        rms = al(pre + "rms", [1, 1024], F32)
        r["rms_b"] = ctx.sb(pre + "rmsb", [1, 1024], BF16)
    t1 = al(pre + "t1", [1, 1024], F32)
    t2 = al(pre + "t2", [1, 1024], F32)
    with nc.Block() as b:
        @b.vector
        def _(v):
            ch = EChain(v, ctx.sem("chV"))
            for h in range(2):
                sl = slice(h * 512, (h + 1) * 512)
                v.tensor_scalar_mul(m[:, sl], ps_s[h][0:1, :], 1.0 / 4096.0)
                v.tensor_scalar_mul(ms[:, sl], ps_q[h][0:1, :], 1.0 / 4096.0)
            ch(v.tensor_mul(t1[:, :], m[:, :], m[:, :]))
            ch(v.tensor_sub(var[:, :], ms[:, :], t1[:, :]))
    with nc.Block() as b:
        @b.scalar
        def _(s):
            ch = EChain(s, ctx.sem("chS"))
            ch(s.activation(t1[:, :], var[:, :], AF.Ln, bias=ctx.cm_eps1, scale=1.0))
            ch(s.activation(rstd[:, :], t1[:, :], AF.Exp, bias=0.0, scale=-0.5))
    with nc.Block() as b:
        @b.vector
        def _(v):
            ch = EChain(v, ctx.sem("chV"))
            ch(v.tensor_mul(t1[:, :], m[:, :], rstd[:, :]))
            ch(v.tensor_scalar_mul(nmr[:, :], t1[:, :], -1.0))
            v.tensor_copy(r["rstd_b"][:, :], rstd[:, :])
            ch(v.tensor_copy(r["nmr_b"][:, :], nmr[:, :]))
            if want_rms:
                ch(v.tensor_scalar_add(t1[:, :], var[:, :], EPS))
                ch(v.reciprocal(t2[:, :], t1[:, :]))
                ch(v.tensor_mul(rms[:, :], var[:, :], t2[:, :]))
    if want_rms:
        with nc.Block() as b:
            @b.scalar
            def _(s):
                ch = EChain(s, ctx.sem("chS"))
                ch(s.activation(t1[:, :], rms[:, :], AF.Ln, bias=ctx.cm_eps1, scale=1.0))
                ch(s.activation(t2[:, :], t1[:, :], AF.Exp, bias=0.0, scale=-0.5))
        with nc.Block() as b:
            @b.vector
            def _(v):
                v.tensor_copy(r["rms_b"][:, :], t2[:, :])
    return r


def bcast_rows(ctx, rows_bf, pre, pb, out_dtype=BF16, width=1024):
    nc = ctx.nc
    outs = [ctx.sb(pre + f"bc{i}", [128, width], out_dtype) for i in range(len(rows_bf))]
    for i, row in enumerate(rows_bf):
        for h in range(width // 512):
            with nc.Block() as b:
                @b.tensor
                def _(t, i=i, h=h):
                    t.matmul(pb[:, :], ctx.cm["onescol"][0:1, :],
                             rows_bf[i][0:1, h * 512:(h + 1) * 512], start=True, stop=True)
            with nc.Block() as b:
                @b.vector
                def _(v, i=i, h=h):
                    v.tensor_copy(outs[i][:, h * 512:(h + 1) * 512], pb[:, :])
    return outs



def build_g1(ctx=None):
    ctx = ctx if ctx is not None else Ctx("g1")
    ctx.begin_group("g1")
    nc = ctx.nc
    x0T = ctx.din("x0T", [4096, 1024])
    wqkv_l = ctx.din("wqkv_l", [3072, 4096])
    wv_r = ctx.din("wv_r", [4096, 1024])
    bqkv_c = ctx.din("bqkv_c", [128, 24])
    bv_row = ctx.din("bv_row", [1, 1024])
    wg0_r = ctx.din("wg0_r", [4096, 8])
    a0_l = ctx.din("a0_l", [128, 4096])
    eexp1 = ctx.din("eexp1", [8, 128])

    xn0d = ctx.dout("xn0d", [4096, 1024], BF16)
    qd = ctx.dout("qd", [1024, 1024], BF16)
    kvctb = ctx.dout("kvctb", [2048, 1024], BF16)
    hw0d = ctx.dout("hw0d", [128, 1024], BF16)
    if DBG:
        routefd = ctx.dout("routefd", [8, 1024], F32)
        rtmd = ctx.dout("rtmd", [128, 64], BF16)
        rted = ctx.dout("rted", [128, 64], F32)
        wfbd = ctx.dout("wfbd", [128, 1024], BF16)
        hrawd = ctx.dout("hrawd", [128, 1024], F32)

    cm = setup_common(ctx)
    dA = ctx.sem("dA")
    dW = ctx.sem("dW")
    dO = ctx.sem("dO")
    sP = ctx.sem("sP")

    ln_es = ExitStack()
    x0buf = [ctx.sbx(ln_es, f"x0buf{i}", [128, 1024], F32) for i in range(2)]
    xb = [ctx.sbx(ln_es, f"xb{i}", [128, 1024], BF16) for i in range(2)]
    sqb = [ctx.sbx(ln_es, f"sqb{i}", [128, 1024], BF16) for i in range(2)]
    xn0 = [ctx.sb(f"xn0_{i}", [128, 1024], BF16) for i in range(32)]
    tbuf = [ctx.sbx(ln_es, f"tbuf{i}", [128, 1024], F32) for i in range(2)]

    bqc = ctx.sb("bqc", [128, 24], F32)
    bvr = ctx.sb("bvr", [1, 1024], BF16)
    eexpb = ctx.sb("eexpb", [8, 128], BF16)
    x0mark = {}
    with nc.Block() as b:
        @b.sync
        def _(sy):
            dA.dinc(sy.dma_start(bqc[:, :], bqkv_c))
            x0mark[0] = dA.dinc(sy.dma_start(x0buf[0][:, :], x0T[0:128, :]))

        @b.gpsimd
        def _(gp):
            dW.dinc(gp.dma_start(bvr[:, :], bv_row))
            dW.dinc(gp.dma_start(eexpb[:, :], eexp1))

    with ExitStack() as pes:
        ps_s = [pes.enter_context(nc.psum_tensor(f"ps_s{h}", [128, 512], F32)) for h in range(2)]
        ps_q = [pes.enter_context(nc.psum_tensor(f"ps_q{h}", [128, 512], F32)) for h in range(2)]
        for i in range(33):
            with nc.Block() as b:
                if i < 32:
                    @b.sync
                    def _(sy, i=i):
                        if i + 1 < 32:
                            x0mark[i + 1] = dA.dinc(sy.dma_start(
                                x0buf[(i + 1) % 2][:, :], x0T[(i + 1) * 128:(i + 2) * 128, :]))

                    @b.vector
                    def _(v, i=i):
                        v.wait_ge(dA.h, x0mark[i])
                        v.tensor_copy(xb[i % 2][:, :], x0buf[i % 2][:, :])

                    @b.scalar
                    def _(s, i=i):
                        s.wait_ge(dA.h, x0mark[i])
                        s.square(sqb[i % 2][:, :], x0buf[i % 2][:, :])
                if i >= 1:
                    @b.tensor
                    def _(t, i=i):
                        j = i - 1
                        for h in range(2):
                            sl = slice(h * 512, (h + 1) * 512)
                            t.matmul(ps_s[h][:, :], cm["ones_bf"][:, :], xb[j % 2][:, sl],
                                     start=(j == 0), stop=(j == 31))
                            t.matmul(ps_q[h][:, :], cm["ones_bf"][:, :], sqb[j % 2][:, sl],
                                     start=(j == 0), stop=(j == 31))
        rows = ln_rows(ctx, ps_s, ps_q, "l0")

    with ExitStack() as pes:
        pbq = pes.enter_context(nc.psum_tensor("pbq", [128, 512], F32))
        arow_b, brow_b = bcast_rows(ctx, [rows["rstd_b"], rows["nmr_b"]], "l0", pbq)

    for i in range(35):
        with nc.Block() as b:
            @b.sync
            def _(sy, i=i):
                if i < 32:
                    x0mark[100 + i] = dA.dinc(sy.dma_start(
                        x0buf[i % 2][:, :], x0T[i * 128:(i + 1) * 128, :]))
                if i >= 3:
                    dO.dinc(sy.dma_start(xn0d[(i - 3) * 128:(i - 2) * 128, :], xn0[i - 3][:, :]))

            if 1 <= i <= 32:
                @b.vector
                def _(v, i=i):
                    j = i - 1
                    v.wait_ge(dA.h, x0mark[100 + j])
                    v.tensor_tensor(tbuf[j % 2][:, :], x0buf[j % 2][:, :], arow_b[:, :], ALU.mult)
            if 2 <= i <= 33:
                @b.gpsimd
                def _(gp, i=i):
                    j = i - 2
                    gp.tensor_add(xn0[j][:, :], tbuf[j % 2][:, :], brow_b[:, :])

    ln_es.close()

    wsl = [ctx.sb(f"wsl{i}", [128, 8192], BF16) for i in range(2)]
    qkbuf = [ctx.sb(f"qkbuf{i}", [128, 1024], BF16) for i in range(2)]
    slmark = {}
    with ExitStack() as pes:
        pp = [pes.enter_context(nc.psum_tensor(f"pp{i}", [128, 1024], F32)) for i in range(2)]
        with nc.Block() as b:
            @b.gpsimd
            def _(gp):
                slmark[0] = dW.dinc(gp.dma_start(
                    wsl[0][:, :], slab_ap(wqkv_l, 0, 2, 4096)))

        for i in range(18):
            with nc.Block(no_gpsimd_drain=True) as b:
                @b.gpsimd
                def _(gp, i=i):
                    if i % 2 == 0 and i // 2 + 1 < 8:
                        sl = i // 2 + 1
                        slmark[sl] = dW.dinc(gp.dma_start(
                            wsl[sl % 2][:, :], slab_ap(wqkv_l, sl * 256, 2, 4096)))

                if i < 16:
                    @b.tensor
                    def _(t, i=i):
                        t.wait_ge(dW.h, slmark[i // 2])
                        sl = wsl[(i // 2) % 2]
                        off = (i % 2) * 4096
                        for kc in range(32):
                            for h in range(2):
                                t.matmul(pp[i % 2][:, h * 512:(h + 1) * 512],
                                         sl[:, off + kc * 128:off + (kc + 1) * 128],
                                         xn0[kc][:, h * 512:(h + 1) * 512],
                                         start=(kc == 0), stop=(kc == 31))
                if 1 <= i <= 16:
                    @b.vector
                    def _(v, i=i):
                        j = i - 1
                        for h in range(2):
                            sl = slice(h * 512, (h + 1) * 512)
                            v.tensor_scalar(qkbuf[j % 2][:, sl], pp[j % 2][:, sl],
                                            bqc[:, j:j + 1], None, ALU.add)
                if 2 <= i:
                    @b.sync
                    def _(sy, i=i):
                        j = i - 2
                        if j < 8:
                            dO.dinc(sy.dma_start(qd[j * 128:(j + 1) * 128, :], qkbuf[j % 2][:, :]))
                        else:
                            dO.dinc(sy.dma_start(kvctb[(j - 8) * 128:(j - 7) * 128, :],
                                                 qkbuf[j % 2][:, :]))

    wvsb = [ctx.sb(f"wvsb{k}", [128, 1024], BF16) for k in range(4)]
    vbuf = [ctx.sb(f"vbuf{i}", [128, 1024], BF16) for i in range(4)]
    wvmark = {}
    with ExitStack() as pes:
        pv = [pes.enter_context(nc.psum_tensor(f"pv{j}", [128, 1024], F32)) for j in range(4)]
        for grp in range(2):
            with nc.Block() as b:
                @b.gpsimd
                def _(gp, grp=grp):
                    for k in range(2):
                        wvmark[(grp, k)] = dW.dinc(gp.dma_start(
                            wvsb[k % 4][:, :], wv_r[k * 128:(k + 1) * 128, :]))
            for kc in range(32):
                with nc.Block(no_gpsimd_drain=True) as b:
                    @b.gpsimd
                    def _(gp, grp=grp, kc=kc):
                        if kc + 2 < 32:
                            wvmark[(grp, kc + 2)] = dW.dinc(gp.dma_start(
                                wvsb[(kc + 2) % 4][:, :],
                                wv_r[(kc + 2) * 128:(kc + 3) * 128, :]))

                    @b.tensor
                    def _(t, grp=grp, kc=kc):
                        t.wait_ge(dW.h, wvmark[(grp, kc)])
                        for j in range(4):
                            tt = grp * 4 + j
                            for h in range(2):
                                t.matmul(pv[j][:, h * 512:(h + 1) * 512],
                                         xn0[kc][:, tt * 128:(tt + 1) * 128],
                                         wvsb[kc % 4][:, h * 512:(h + 1) * 512],
                                         start=(kc == 0), stop=False)
                        if kc == 31:
                            for j in range(4):
                                for h in range(2):
                                    t.matmul(pv[j][:, h * 512:(h + 1) * 512],
                                             cm["onescol"][0:1, :],
                                             bvr[0:1, h * 512:(h + 1) * 512],
                                             start=False, stop=True)
            with nc.Block() as b:
                @b.vector
                def _(v, grp=grp):
                    for j in range(4):
                        v.tensor_copy(vbuf[j][:, :], pv[j][:, :])
            with nc.Block() as b:
                @b.sync
                def _(sy, grp=grp):
                    for j in range(4):
                        tt = grp * 4 + j
                        dO.dinc(sy.dma_start(
                            kvctb[1024 + tt * 128:1024 + (tt + 1) * 128, :], vbuf[j][:, :]))

    wg0b = ctx.sb("wg0b", [128, 256], BF16)
    with nc.Block() as b:
        @b.gpsimd
        def _(gp):
            dW.dinc(gp.dma_start(wg0b[:, :], bass.AP(wg0_r.tensor, 0, [[8, 128], [128 * 8, 32], [1, 8]])))

    routef = ctx.sb("routef", [8, 1024], F32)
    rtpad = ctx.sb("rtpad", [128, 128], BF16)
    route_tm = ctx.sb("route_tm", [128, 8], BF16)
    rt_e = ctx.sb("rt_e", [128, 8], F32)
    rt_t = ctx.sb("rt_t", [128, 8], F32)
    rt_u = ctx.sb("rt_u", [128, 8], F32)
    rcol = [ctx.sb(f"rcol{i}", [128, 1], F32) for i in range(4)]
    wg_mark = dW.v

    with ExitStack() as pes:
        plg = pes.enter_context(nc.psum_tensor("plg", [128, 512], F32))
        prt = pes.enter_context(nc.psum_tensor("prt", [128, 128], F32))
        with nc.Block() as b:
            @b.gpsimd
            def _(gp):
                gp.memset(rtpad[:, :], 0.0)

        for tt in range(8):
            with nc.Block() as b:
                @b.tensor
                def _(t, tt=tt):
                    t.wait_ge(dW.h, wg_mark)
                    for kc in range(32):
                        t.matmul(plg[:, 0:8], xn0[kc][:, tt * 128:(tt + 1) * 128],
                                 wg0b[:, kc * 8:(kc + 1) * 8], start=(kc == 0), stop=(kc == 31))
            with nc.Block() as b:
                @b.vector
                def _(v, tt=tt):
                    ch = EChain(v, ctx.sem("chV"))
                    ch(v.reduce_max(rcol[0][:, :], plg[:, 0:8], AXX))
                    ch(v.tensor_scalar_mul(rcol[1][:, :], rcol[0][:, :], -1.0))
            with nc.Block() as b:
                @b.scalar
                def _(s, tt=tt):
                    s.activation(rt_e[:, :], plg[:, 0:8], AF.Exp, bias=rcol[1][:, 0:1])
            with nc.Block() as b:
                @b.vector
                def _(v, tt=tt):
                    ch = EChain(v, ctx.sem("chV"))
                    e = rt_e
                    ch(v.reduce_max(rcol[1][:, :], e[:, :], AXX))
                    ch(v.tensor_scalar(rt_t[:, :], e[:, :], rcol[1][:, 0:1], None, ALU.is_ge))
                    ch(v.scalar_tensor_tensor(rt_u[:, :], rt_t[:, :], -10.0, e[:, :], ALU.mult, ALU.add))
                    ch(v.reduce_max(rcol[2][:, :], rt_u[:, :], AXX))
                    ch(v.tensor_scalar(rt_t[:, :], e[:, :], rcol[2][:, 0:1], None, ALU.is_ge))
                    ch(v.tensor_mul(rt_u[:, :], rt_t[:, :], e[:, :]))
                    ch(v.reduce_sum(rcol[3][:, :], rt_u[:, :], AXX))
                    ch(v.reciprocal(rcol[0][:, :], rcol[3][:, 0:1]))
                    ch(v.tensor_scalar(route_tm[:, :], rt_u[:, :], rcol[0][:, 0:1], None, ALU.mult))
                    ch(v.tensor_copy(rtpad[:, 0:8], route_tm[:, :]))
            with nc.Block() as b:
                @b.tensor
                def _(t, tt=tt):
                    t.matmul(prt[:, :], rtpad[:, :], cm["identb"][:, :], start=True, stop=True)
            with nc.Block() as b:
                @b.scalar
                def _(s, tt=tt):
                    s.copy(routef[:, tt * 128:(tt + 1) * 128], prt[0:8, :])
            if DBG:
                with nc.Block() as b:
                    @b.sync
                    def _(sy, tt=tt):
                        dO.dinc(sy.dma_start(rtmd[:, tt * 8:(tt + 1) * 8], route_tm[:, :]))
                        dO.dinc(sy.dma_start(rted[:, tt * 8:(tt + 1) * 8], rt_e[:, :]))

    rfb = ctx.sb("rfb", [8, 1024], BF16)
    wfb = ctx.sb("wfb", [128, 1024], BF16)
    a0sl = ctx.sb("a0sl", [128, 4096], BF16)
    hw0b = ctx.sb("hw0b", [128, 1024], BF16)
    with ExitStack() as pes:
        pw = [pes.enter_context(nc.psum_tensor(f"pw{i}", [128, 1024], F32)) for i in range(2)]
        with nc.Block() as b:
            @b.vector
            def _(v):
                v.tensor_copy(rfb[:, :], routef[:, :])

            @b.gpsimd
            def _(gp):
                dW.dinc(gp.dma_start(a0sl[:, :], a0_l[0:128, :]))
        a0_mark = dW.v
        with nc.Block() as b:
            @b.tensor
            def _(t):
                for h in range(2):
                    t.matmul(pw[0][:, h * 512:(h + 1) * 512], eexpb[:, :],
                             rfb[0:8, h * 512:(h + 1) * 512], start=True, stop=True)
        with nc.Block() as b:
            @b.vector
            def _(v):
                v.tensor_copy(wfb[:, :], pw[0][:, :])
        with nc.Block() as b:
            @b.tensor
            def _(t):
                t.wait_ge(dW.h, a0_mark)
                for kc in range(32):
                    for h in range(2):
                        t.matmul(pw[1][:, h * 512:(h + 1) * 512],
                                 a0sl[:, kc * 128:(kc + 1) * 128],
                                 xn0[kc][:, h * 512:(h + 1) * 512],
                                 start=(kc == 0), stop=(kc == 31))
        if DBG:
            hraw = ctx.sb("hraw", [128, 1024], F32)
            with nc.Block() as b:
                @b.vector
                def _(v):
                    v.tensor_copy(hraw[:, :], pw[1][:, :])
            with nc.Block() as b:
                @b.sync
                def _(sy):
                    dO.dinc(sy.dma_start(routefd, routef[:, :]))
                    dO.dinc(sy.dma_start(wfbd, wfb[:, :]))
                    dO.dinc(sy.dma_start(hrawd, hraw[:, :]))
        with nc.Block() as b:
            @b.vector
            def _(v):
                v.tensor_mul(hw0b[:, :], pw[1][:, :], wfb[:, :])
    with nc.Block() as b:
        @b.sync
        def _(sy):
            dO.dinc(sy.dma_start(hw0d, hw0b[:, :]))
            sy.wait_ge(dO.h, dO.v)
            sy.wait_ge(dA.h, dA.v)

        @b.gpsimd
        def _(gp):
            gp.wait_ge(dW.h, dW.v)
    ctx.end_group()
    return ctx





def build_g2(ctx=None):
    ctx = ctx if ctx is not None else Ctx("g2")
    ctx.begin_group("g2")
    nc = ctx.nc
    qd = ctx.din("qd", [1024, 1024], BF16)
    kvctb = ctx.din("kvctb", [2048, 1024], BF16)
    od = ctx.dout("od", [1024, 1024], BF16)
    if DBG:
        pnd = ctx.dout("pnd", [128, 4096], BF16)
        ptd = ctx.dout("ptd", [128, 16384], BF16)
        mxad = ctx.dout("mxad", [128, 2], F32)
    kvci = ctx.dint("kvci", [2048, 1024], BF16)
    kvg = ctx.dint("kvg", [8192, 1024], BF16)

    cm = setup_common(ctx)
    dA = ctx.sem("dA")
    dO = ctx.sem("dO")
    sC = ctx.sem("sC")
    sP = ctx.sem("sP")

    Q = [ctx.sb(f"Q{i}", [128, 1024], BF16) for i in range(8)]
    Ksb = [ctx.sb(f"Ksb{i}", [128, 4096], BF16) for i in range(8)]
    PT = ctx.sb("PT", [128, 16384], BF16)
    Pexp = ctx.sb("Pexp", [128, 4096], BF16)
    Pn = ctx.sb("Pn", [128, 4096], BF16)
    mx8 = ctx.sb("mx8", [128, 8], F32)
    nmx = ctx.sb("nmx", [128, 1], F32)
    mxa = ctx.sb("mxa", [128, 1], F32)
    smc = [ctx.sb(f"smc{i}", [128, 1], F32) for i in range(8)]
    rP = ctx.sb("rP", [128, 1], F32)
    Vsb = [ctx.sb(f"Vsb{i}", [128, 1024], BF16) for i in range(4)]
    obuf = [ctx.sb(f"obuf{i}", [128, 512], BF16) for i in range(8)]

    kmark = {}
    with nc.Block() as b:
        @b.gpsimd
        def _(gp):
            dA.dinc(gp.dma_start(kvci, kvctb))
            gp.wait_ge(dA.h, dA.v)
            sC.inc(gp.collective_compute(
                "AllGather", ALU.bypass, GROUPS, [kvci.opt()], [kvg.opt()]))

        @b.sync
        def _(sy):
            for i in range(8):
                dA.dinc(sy.dma_start(Q[i][:, :], qd[i * 128:(i + 1) * 128, :]))
            sy.wait_ge(sC.h, sC.v)
            for dch in range(8):
                for rk in range(4):
                    kmark[(dch, rk)] = dA.dinc(sy.dma_start(
                        Ksb[dch][:, rk * 1024:(rk + 1) * 1024],
                        kvg[rk * 2048 + dch * 128:rk * 2048 + dch * 128 + 128, :]))
    k_ready = dA.v

    vmark = {}
    for half in range(2):
        for j in range(4):
            qt = half * 4 + j
            with ExitStack() as pes:
                psc = [pes.enter_context(nc.psum_tensor(ctx.uname(f"psc{kb}"), [128, 512], F32))
                       for kb in range(8)]
                with nc.Block() as b:
                    @b.tensor
                    def _(t, qt=qt):
                        t.wait_ge(dA.h, k_ready)
                        for dch in range(8):
                            for kb in range(8):
                                t.matmul(psc[kb][:, :], Q[dch][:, qt * 128:(qt + 1) * 128],
                                         Ksb[dch][:, kb * 512:(kb + 1) * 512],
                                         start=(dch == 0), stop=(dch == 7))
                with nc.Block() as b:
                    @b.vector
                    def _(v, qt=qt):
                        ch = EChain(v, ctx.sem("chV"))
                        for kb in range(7):
                            v.reduce_max(mx8[:, kb:kb + 1], psc[kb][:, :], AXX)
                        ch(v.reduce_max(mx8[:, 7:8], psc[7][:, :], AXX))
                        ch(v.reduce_max(mxa[:, :], mx8[:, :], AXX))
                        ch(v.tensor_scalar_mul(nmx[:, :], mxa[:, :], -1.0 / 32.0))
                with nc.Block() as b:
                    @b.scalar
                    def _(s, qt=qt):
                        for kb in range(8):
                            s.activation(Pexp[:, kb * 512:(kb + 1) * 512], psc[kb][:, :],
                                         AF.Exp, bias=nmx[:, 0:1], scale=1.0 / 32.0,
                                         accum_out=smc[kb][:, 0:1])
            with nc.Block() as b:
                @b.vector
                def _(v, qt=qt):
                    ch = EChain(v, ctx.sem("chV"))
                    v.tensor_add(mx8[:, 0:1], smc[0][:, :], smc[1][:, :])
                    v.tensor_add(mx8[:, 1:2], smc[2][:, :], smc[3][:, :])
                    v.tensor_add(mx8[:, 2:3], smc[4][:, :], smc[5][:, :])
                    ch(v.tensor_add(mx8[:, 3:4], smc[6][:, :], smc[7][:, :]))
                    ch(v.tensor_add(mx8[:, 4:5], mx8[:, 0:1], mx8[:, 1:2]))
                    ch(v.tensor_add(mx8[:, 5:6], mx8[:, 2:3], mx8[:, 3:4]))
                    ch(v.tensor_add(mx8[:, 6:7], mx8[:, 4:5], mx8[:, 5:6]))
                    ch(v.reciprocal(rP[:, :], mx8[:, 6:7]))
                    ch(v.tensor_scalar(Pn[:, :], Pexp[:, :], rP[:, 0:1], None, ALU.mult))
            if DBG and qt == 0:
                with nc.Block() as b:
                    @b.sync
                    def _(sy):
                        dO.dinc(sy.dma_start(pnd, Pn[:, :]))
                        dO.dinc(sy.dma_start(mxad[:, 0:1], mxa[:, :], allow_slow_non_contiguous=True))
                        dO.dinc(sy.dma_start(mxad[:, 1:2], rP[:, :], allow_slow_non_contiguous=True))
            with ExitStack() as pes:
                ptp = [pes.enter_context(nc.psum_tensor(ctx.uname(f"ptp{g}"), [128, 1024], BF16))
                       for g in range(4)]
                with nc.Block() as b:
                    @b.tensor
                    def _(t, qt=qt, j=j):
                        for kc in range(32):
                            t.transpose(ptp[kc // 8][:, (kc % 8) * 128:(kc % 8 + 1) * 128],
                                        Pn[:, kc * 128:(kc + 1) * 128], cm["identb"][:, :])
                with nc.Block() as b:
                    @b.vector
                    def _(v, qt=qt, j=j):
                        for g in range(4):
                            col0 = (g * 8) * 512 + j * 128
                            dst = bass.AP(PT, PT[:, col0:col0 + 1].offset,
                                          [list(PT[:, :].ap[0]), [512, 8], [1, 128]])
                            v.tensor_copy(dst, ptp[g][:, :].rearrange("p (a c) -> p a c", c=128))
        if DBG and half == 0:
            with nc.Block() as b:
                @b.sync
                def _(sy):
                    dO.dinc(sy.dma_start(ptd, PT[:, :]))
        with ExitStack() as pes:
            pso = [pes.enter_context(nc.psum_tensor(ctx.uname(f"pso{d}"), [128, 512], F32))
                   for d in range(8)]
            with nc.Block() as b:
                @b.sync
                def _(sy, half=half):
                    for k in range(2):
                        rk = k // 8
                        vmark[(half, k)] = dA.dinc(sy.dma_start(
                            Vsb[k % 4][:, :],
                            kvg[rk * 2048 + 1024 + (k % 8) * 128:rk * 2048 + 1024 + (k % 8 + 1) * 128, :]))
            for kc in range(32):
                with nc.Block(no_gpsimd_drain=True) as b:
                    @b.sync
                    def _(sy, half=half, kc=kc):
                        if kc + 2 < 32:
                            k = kc + 2
                            rk = k // 8
                            vmark[(half, k)] = dA.dinc(sy.dma_start(
                                Vsb[k % 4][:, :],
                                kvg[rk * 2048 + 1024 + (k % 8) * 128:rk * 2048 + 1024 + (k % 8 + 1) * 128, :]))

                    @b.tensor
                    def _(t, half=half, kc=kc):
                        t.wait_ge(dA.h, vmark[(half, kc)])
                        for dch in range(8):
                            t.matmul(pso[dch][:, :], Vsb[kc % 4][:, dch * 128:(dch + 1) * 128],
                                     PT[:, kc * 512:(kc + 1) * 512],
                                     start=(kc == 0), stop=(kc == 31))
            for i in range(10):
                with nc.Block() as b:
                    if i < 8:
                        @b.vector
                        def _(v, half=half, i=i):
                            v.tensor_copy(obuf[i][:, :], pso[i][:, :])
                    if 1 <= i <= 8:
                        @b.sync
                        def _(sy, half=half, i=i):
                            j = i - 1
                            dO.dinc(sy.dma_start(
                                od[j * 128:(j + 1) * 128, half * 512:(half + 1) * 512],
                                obuf[j][:, :]))
    with nc.Block() as b:
        @b.sync
        def _(sy):
            sy.wait_ge(dO.h, dO.v)
            sy.wait_ge(dA.h, dA.v)
    ctx.end_group()
    return ctx




def build_g3(ctx=None):
    ctx = ctx if ctx is not None else Ctx("g3")
    ctx.begin_group("g3")
    nc = ctx.nc
    x0T = ctx.din("x0T", [4096, 1024])
    xn0d = ctx.din("xn0d", [4096, 1024], BF16)
    od = ctx.din("od", [1024, 1024], BF16)
    hw0d = ctx.din("hw0d", [128, 1024], BF16)
    wo_l = ctx.din("wo_l", [4096, 1024])
    wb0_l = ctx.din("wb0_l", [4096, 4096])
    b0f_l = ctx.din("b0f_l", [4096, 128])
    b01_c = ctx.din("b01_c", [128, 32])
    nw_c = ctx.din("nw_c", [128, 32])
    wip_l = ctx.din("wip_l", [18560, 4096])
    convw_c = ctx.din("convw_c", [128, 320])
    convb_c = ctx.din("convb_c", [128, 80])
    dtb_c = ctx.din("dtb_c", [128, 1])
    alog_c = ctx.din("alog_c", [128, 1])
    halo_sel = ctx.din("halo_sel", [128, 1])

    x1d = ctx.dout("x1d", [4096, 1024], F32)
    xn1d = ctx.dout("xn1d", [4096, 1024], F32)
    hidd = ctx.dout("hidd", [8, 64, 128, 128], BF16)
    gated = ctx.dout("gated", [8, 64, 128, 128], BF16)
    bcd = ctx.dout("bcd", [16, 128, 1024], BF16)
    smalls = ctx.dout("smalls", [128, 3104], F32)
    hfixd = ctx.dout("hfixd", [4, 8192], BF16)
    if DBG:
        xn1sd = ctx.dout("xn1sd", [4096, 1024], BF16)
        dtrawd = ctx.dout("dtrawd", [128, 1024], F32)
    cc1i = ctx.dint("cc1i", [128, 240], BF16)
    cc1g = ctx.dint("cc1g", [512, 240], BF16)

    cm = setup_common(ctx)
    dA = ctx.sem("dA")
    dW = ctx.sem("dW")
    dO = ctx.sem("dO")
    sC = ctx.sem("sC")

    b01 = ctx.sb("b01", [128, 32], F32)
    nwc = ctx.sb("nwc", [128, 32], BF16)
    cvw = ctx.sb("cvw", [128, 320], F32)
    cvb = ctx.sb("cvb", [128, 80], F32)
    dtb = ctx.sb("dtb", [128, 1], F32)
    alog = ctx.sb("alog", [128, 1], F32)
    hsel = ctx.sb("hsel", [128, 1], F32)
    onef = ctx.sb("onef", [128, 1], F32)
    with nc.Block() as b:
        @b.sync
        def _(sy):
            dA.dinc(sy.dma_start(b01[:, :], b01_c))
            dA.dinc(sy.dma_start(cvw[:, :], convw_c))
            dA.dinc(sy.dma_start(cvb[:, :], convb_c))
            dA.dinc(sy.dma_start(dtb[:, :], dtb_c))
            dA.dinc(sy.dma_start(alog[:, :], alog_c))
            dA.dinc(sy.dma_start(hsel[:, :], halo_sel))

        @b.gpsimd
        def _(gp):
            gp.memset(onef[:, :], 1.0)
            dW.dinc(gp.dma_start(nwc[:, :], nw_c))

    bes = ExitStack()
    x0buf = [ctx.sbx(bes, f"x0buf{i}", [128, 1024], F32) for i in range(4)]
    tbuf = [ctx.sbx(bes, f"tbufC{i}", [128, 1024], F32) for i in range(2)]
    xn1f = [ctx.sbx(bes, f"xn1f{i}", [128, 1024], F32) for i in range(4)]
    bes2 = ExitStack()
    xn0 = [ctx.sbx(bes2, f"xn0_{i}", [128, 1024], BF16) for i in range(32)]
    Osb = [ctx.sbx(bes2, f"Osb{i}", [128, 1024], BF16) for i in range(8)]
    hw0 = ctx.sbx(bes2, "hw0", [128, 1024], BF16)
    x1buf = [ctx.sbx(bes2, f"x1buf{i}", [128, 1024], F32) for i in range(4)]
    xb1 = [ctx.sbx(bes2, f"xb1_{i}", [128, 1024], BF16) for i in range(2)]
    sqb = [ctx.sbx(bes2, f"sqb{i}", [128, 1024], BF16) for i in range(2)]
    wsl = [ctx.sbx(bes2, f"wslB{i}", [128, 4096], BF16) for i in range(2)]
    wosl = [ctx.sbx(bes2, f"wosl{i}", [128, 1024], BF16) for i in range(2)]
    bfsl = [ctx.sbx(bes2, f"bfsl{i}", [128, 128], BF16) for i in range(2)]
    x0mark = {}
    slm = {}
    with nc.Block() as b:
        @b.sync
        def _(sy):
            for i in range(32):
                dA.dinc(sy.dma_start(xn0[i][:, :], xn0d[i * 128:(i + 1) * 128, :]))
            for i in range(8):
                dA.dinc(sy.dma_start(Osb[i][:, :], od[i * 128:(i + 1) * 128, :]))
            dA.dinc(sy.dma_start(hw0[:, :], hw0d))
            x0mark[0] = dA.dinc(sy.dma_start(x0buf[0][:, :], x0T[0:128, :]))
        @b.gpsimd
        def _(gp):
            slm[0] = dW.dinc(gp.dma_start(wsl[0][:, :], slab_ap(wb0_l, 0, 1, 4096)))
            slm[(1, 0)] = dW.dinc(gp.dma_start(wosl[0][:, :], slab_ap(wo_l, 0, 1, 1024)))
            slm[(2, 0)] = dW.dinc(gp.dma_start(bfsl[0][:, :], slab_ap(b0f_l, 0, 1, 128)))
    b_ready = dA.v

    pes = ExitStack()
    pp = [pes.enter_context(nc.psum_tensor("ppB0", [128, 1024], F32)),
          pes.enter_context(nc.psum_tensor("ppB1", [128, 1024], F32))]
    ps_s = [pes.enter_context(nc.psum_tensor(f"ps_sB{h}", [128, 512], F32)) for h in range(2)]
    ps_q = [pes.enter_context(nc.psum_tensor(f"ps_qB{h}", [128, 512], F32)) for h in range(2)]
    for i in range(36):
        with nc.Block() as b:
            @b.sync
            def _(sy, i=i):
                if i + 1 < 32:
                    x0mark[i + 1] = dA.dinc(sy.dma_start(
                        x0buf[(i + 1) % 4][:, :], x0T[(i + 1) * 128:(i + 2) * 128, :]))
                if 2 <= i < 34:
                    j = i - 2
                    dO.dinc(sy.dma_start(x1d[j * 128:(j + 1) * 128, :], x1buf[j % 4][:, :]))

            @b.gpsimd
            def _(gp, i=i):
                if i + 1 < 32:
                    sl = i + 1
                    slm[sl] = dW.dinc(gp.dma_start(wsl[sl % 2][:, :], slab_ap(wb0_l, sl * 128, 1, 4096)))
                if i + 1 < 32:
                    slm[(1, i + 1)] = dW.dinc(gp.dma_start(
                        wosl[(i + 1) % 2][:, :], slab_ap(wo_l, (i + 1) * 128, 1, 1024)))
                    slm[(2, i + 1)] = dW.dinc(gp.dma_start(
                        bfsl[(i + 1) % 2][:, :], slab_ap(b0f_l, (i + 1) * 128, 1, 128)))
                if 2 <= i < 34:
                    j = i - 2
                    gp.tensor_copy(xb1[j % 2][:, :], x1buf[j % 4][:, :])

            if i < 32 or 3 <= i <= 34:
                @b.tensor
                def _(t, i=i):
                    if i < 32:
                        t.wait_ge(dW.h, slm[i])
                        t.wait_ge(dW.h, slm[(1, i)])
                        t.wait_ge(dW.h, slm[(2, i)])
                        t.wait_ge(dA.h, b_ready)
                        for kc in range(32):
                            for h in range(2):
                                hs = slice(h * 512, (h + 1) * 512)
                                t.matmul(pp[i % 2][:, hs], wsl[i % 2][:, kc * 128:(kc + 1) * 128],
                                         xn0[kc][:, hs], start=(kc == 0), stop=False)
                        for kc in range(8):
                            for h in range(2):
                                hs = slice(h * 512, (h + 1) * 512)
                                t.matmul(pp[i % 2][:, hs], wosl[i % 2][:, kc * 128:(kc + 1) * 128],
                                         Osb[kc][:, hs], start=False, stop=False)
                        for h in range(2):
                            hs = slice(h * 512, (h + 1) * 512)
                            t.matmul(pp[i % 2][:, hs], bfsl[i % 2][:, :], hw0[:, hs],
                                     start=False, stop=(h == 1) or True)
                    if 3 <= i <= 34:
                        j = i - 3
                        for h in range(2):
                            hs = slice(h * 512, (h + 1) * 512)
                            t.matmul(ps_s[h][:, :], cm["ones_bf"][:, :], xb1[j % 2][:, hs],
                                     start=(j == 0), stop=(j == 31))
                            t.matmul(ps_q[h][:, :], cm["ones_bf"][:, :], sqb[j % 2][:, hs],
                                     start=(j == 0), stop=(j == 31))
            if 1 <= i <= 32:
                @b.vector
                def _(v, i=i):
                    j = i - 1
                    v.wait_ge(dA.h, x0mark[j])
                    for h in range(2):
                        hs = slice(h * 512, (h + 1) * 512)
                        v.scalar_tensor_tensor(x1buf[j % 4][:, hs], pp[j % 2][:, hs],
                                               b01[:, j:j + 1], x0buf[j % 4][:, hs],
                                               ALU.add, ALU.add)
            if 2 <= i <= 33:
                @b.scalar
                def _(s, i=i):
                    j = i - 2
                    s.square(sqb[j % 2][:, :], x1buf[j % 4][:, :])
    rows1 = ln_rows(ctx, ps_s, ps_q, "l1", want_rms=True, stack=bes2)
    pes.close()
    bes2.close()
    with ExitStack() as pes2:
        pbq = pes2.enter_context(nc.psum_tensor("pbqB", [128, 512], F32))
        bcs = bcast_rows(ctx, [rows1["rstd_b"], rows1["nmr_b"], rows1["rms_b"]], "l1", pbq)
    arow1, brow1, rmsb = bcs

    xn1s = []
    for i in range(32):
        xn1s.append(ctx.sb(f"xn1s_{i}", [128, 1024], BF16))
    for i in range(35):
        with nc.Block() as b:
            @b.sync
            def _(sy, i=i):
                if i < 32:
                    x0mark[200 + i] = dA.dinc(sy.dma_start(
                        x0buf[i % 4][:, :], x1d[i * 128:(i + 1) * 128, :]))
                if 3 <= i:
                    j = i - 3
                    dO.dinc(sy.dma_start(xn1d[j * 128:(j + 1) * 128, :], xn1f[j % 4][:, :]))

            if 1 <= i <= 34:
                @b.vector
                def _(v, i=i):
                    if 1 <= i <= 32:
                        j = i - 1
                        v.wait_ge(dA.h, x0mark[200 + j])
                        v.tensor_tensor(tbuf[j % 2][:, :], x0buf[j % 4][:, :], arow1[:, :], ALU.mult)
                    if 3 <= i <= 34:
                        j = i - 3
                        v.scalar_tensor_tensor(xn1s[j][:, :], xn1f[j % 4][:, :], nwc[:, j:j + 1],
                                               rmsb[:, :], ALU.mult, ALU.mult)
            if 2 <= i <= 33:
                @b.gpsimd
                def _(gp, i=i):
                    j = i - 2
                    gp.tensor_add(xn1f[j % 4][:, :], tbuf[j % 2][:, :], brow1[:, :])
    bes.close()

    des = ExitStack()
    wslD = [ctx.sbx(des, f"wslD{i}", [128, 4096], BF16) for i in range(2)]
    hpre = [ctx.sbx(des, f"hpre{i}", [128, 1024], BF16) for i in range(2)]
    tA = [ctx.sbx(des, f"tA{i}", [128, 1024], BF16) for i in range(2)]
    tB = [ctx.sbx(des, f"tB{i}", [128, 1024], BF16) for i in range(2)]
    hcbuf = [ctx.sbx(des, f"hcbuf{i}", [128, 1024], BF16) for i in range(4)]
    bcfm = [ctx.sb(f"bcfm{i}", [128, 1024], BF16) for i in range(16)]
    halo_sb = ctx.sb("halo_sb", [128, 240], BF16)
    own3 = ctx.sb("own3", [128, 240], BF16)
    dtraw = ctx.sb("dtraw", [128, 1024], F32)
    dtf = ctx.sb("dtf", [128, 1024], F32)
    acs = ctx.sb("acs", [128, 1024], F32)
    afull = ctx.sb("afull", [128, 1024], F32)
    acol = ctx.sb("acol", [128, 8], F32)
    tails = ctx.sb("tails", [128, 8], F32)
    abias = ctx.sb("abias", [128, 8], F32)
    dcol = ctx.sb("dcol", [128, 1], F32)
    negA = ctx.sb("negA", [128, 1], F32)
    sc1 = ctx.sb("sc1", [128, 1024], F32)
    sc2 = ctx.sb("sc2", [128, 1024], F32)
    sc3 = ctx.sb("sc3", [128, 1024], F32)

    ocs = [144] + list(range(144))
    slmD = {}

    with nc.Block() as b:
        @b.gpsimd
        def _(gp):
            slmD[0] = dW.dinc(gp.dma_start(wslD[0][:, :], slab_ap(wip_l, 144 * 128, 1, 4096)))

    pesD = ExitStack()
    ppD = [pesD.enter_context(nc.psum_tensor("ppD0", [128, 1024], F32)),
           pesD.enter_context(nc.psum_tensor("ppD1", [128, 1024], F32))]

    cvb0 = [ctx.sb(f"cvb0_{i}", [128, 1], F32) for i in range(2)]
    if DBG:
        with nc.Block() as b:
            @b.sync
            def _(sy):
                for i in range(32):
                    dO.dinc(sy.dma_start(xn1sd[i * 128:(i + 1) * 128, :], xn1s[i][:, :]))
    n = len(ocs)
    for i in range(n + 4):
        with nc.Block(no_gpsimd_drain=True) as b:
            @b.gpsimd
            def _(gp, i=i):
                si = i + 1
                if si < n:
                    slmD[si] = dW.dinc(gp.dma_start(
                        wslD[si % 2][:, :], slab_ap(wip_l, ocs[si] * 128, 1, 4096)))

            if i < n:
                @b.tensor
                def _(t, i=i):
                    t.wait_ge(dW.h, slmD[i])
                    for kc in range(32):
                        for h in range(2):
                            hs = slice(h * 512, (h + 1) * 512)
                            t.matmul(ppD[i % 2][:, hs],
                                     wslD[i % 2][:, kc * 128:(kc + 1) * 128],
                                     xn1s[kc][:, hs], start=(kc == 0), stop=(kc == 31))

            @b.vector
            def _(v, i=i):
                ch = EChain(v, ctx.sem("chV"))
                if 1 <= i <= n:
                    oc = ocs[i - 1]
                    j = i - 1
                    if oc == 144:
                        v.tensor_copy(dtraw[:, :], ppD[j % 2][:, :])
                    elif oc >= 64:
                        cf = oc - 64
                        ch(v.tensor_copy(hpre[j % 2][:, :], ppD[j % 2][:, :]))
                        ch(v.tensor_scalar(tA[j % 2][:, 0:1021], hpre[j % 2][:, 0:1021],
                                           cvw[:, cf * 4:cf * 4 + 1], None, ALU.mult))
                        ch(v.scalar_tensor_tensor(tB[j % 2][:, 0:1021], hpre[j % 2][:, 1:1022],
                                                  cvw[:, cf * 4 + 1:cf * 4 + 2], tA[j % 2][:, 0:1021],
                                                  ALU.mult, ALU.add))
                        ch(v.scalar_tensor_tensor(tA[j % 2][:, 0:1021], hpre[j % 2][:, 2:1023],
                                                  cvw[:, cf * 4 + 2:cf * 4 + 3], tB[j % 2][:, 0:1021],
                                                  ALU.mult, ALU.add))
                        ch(v.scalar_tensor_tensor(tB[j % 2][:, 0:1021], hpre[j % 2][:, 3:1024],
                                                  cvw[:, cf * 4 + 3:cf * 4 + 4], tA[j % 2][:, 0:1021],
                                                  ALU.mult, ALU.add))
                        ch(v.tensor_copy(halo_sb[:, cf * 3:cf * 3 + 3], hpre[j % 2][:, 1021:1024]))
                        ch(v.tensor_copy(own3[:, cf * 3:cf * 3 + 3], hpre[j % 2][:, 0:3]))
                        ch(v.tensor_copy(cvb0[j % 2][:, :], cvb[:, cf:cf + 1]))
                if 3 <= i <= n + 2:
                    oc = ocs[i - 3]
                    j = i - 3
                    if 128 <= oc < 144:
                        v.tensor_copy(bcfm[oc - 128][:, :], hcbuf[j % 4][:, :])

            if 1 <= i <= n + 1:
                @b.scalar
                def _(s, i=i):
                    if 1 <= i <= n:
                        oc = ocs[i - 1]
                        j = i - 1
                        if oc < 64:
                            for h in range(2):
                                hs = slice(h * 512, (h + 1) * 512)
                                s.activation(hcbuf[j % 4][:, hs], ppD[j % 2][:, hs], AF.Silu)
                    if 2 <= i <= n + 1:
                        oc = ocs[i - 2]
                        j = i - 2
                        if oc >= 64 and oc != 144:
                            s.activation(hcbuf[j % 4][:, 3:1024], tB[j % 2][:, 0:1021], AF.Silu,
                                         bias=cvb0[j % 2][:, 0:1])

            if 2 <= i <= n + 3:
                @b.sync
                def _(sy, i=i):
                    if 2 <= i <= n + 1:
                        oc = ocs[i - 2]
                        j = i - 2
                        if oc < 64:
                            for tc in range(8):
                                dO.dinc(sy.dma_start(
                                    gated[tc, oc], hcbuf[j % 4][:, tc * 128:(tc + 1) * 128]))
                    if 3 <= i <= n + 2:
                        oc = ocs[i - 3]
                        j = i - 3
                        if 64 <= oc < 128:
                            for tc in range(8):
                                dO.dinc(sy.dma_start(
                                    hidd[tc, oc - 64], hcbuf[j % 4][:, tc * 128:(tc + 1) * 128]))

    with nc.Block() as b:
        @b.vector
        def _(v):
            ch = EChain(v, ctx.sem("chV"))
            ch(v.tensor_scalar(sc1[:, :], dtraw[:, :], dtb[:, 0:1], None, ALU.add))
            ch(v.tensor_scalar_mul(sc3[:, :], sc1[:, :], -1.0))
            ch(v.tensor_tensor(sc2[:, :], sc1[:, :], sc3[:, :], ALU.min))
    with nc.Block() as b:
        @b.scalar
        def _(s):
            ch = EChain(s, ctx.sem("chS"))
            ch(s.activation(sc3[:, :], sc2[:, :], AF.Exp, bias=0.0, scale=1.0))
            ch(s.activation(sc2[:, :], sc3[:, :], AF.Ln, bias=onef[:, 0:1], scale=1.0))
    with nc.Block() as b:
        @b.vector
        def _(v):
            ch = EChain(v, ctx.sem("chV"))
            ch(v.tensor_relu(sc3[:, :], sc1[:, :]))
            ch(v.tensor_add(dtf[:, :], sc3[:, :], sc2[:, :]))
    with nc.Block() as b:
        @b.scalar
        def _(s):
            s.activation(negA[:, :], alog[:, 0:1], AF.Exp)
    with nc.Block() as b:
        @b.vector
        def _(v):
            ch = EChain(v, ctx.sem("chV"))
            ch(v.tensor_scalar_mul(sc1[:, 0:1], negA[:, 0:1], -1.0))
            ch(v.tensor_scalar(afull[:, :], dtf[:, :], sc1[:, 0:1], None, ALU.mult))
    with nc.Block() as b:
        @b.vector
        def _(v):
            for c in range(8):
                v.tensor_tensor_scan(acs[:, c * 128:(c + 1) * 128], afull[:, c * 128:(c + 1) * 128],
                                     afull[:, c * 128:(c + 1) * 128], 0.0, ALU.add, ALU.bypass)
    with nc.Block() as b:
        @b.vector
        def _(v):
            ch = EChain(v, ctx.sem("chV"))
            src = bass.AP(acs, acs[:, 127:128].offset, [list(acs[:, :].ap[0]), [128, 8]])
            ch(v.tensor_copy(acol[:, :], src))
            ch(v.memset(tails[:, 7:8], 0.0))
            for c in range(6, -1, -1):
                ch(v.tensor_add(tails[:, c:c + 1], tails[:, c + 1:c + 2], acol[:, c + 1:c + 2]))
            ch(v.tensor_add(abias[:, :], acol[:, :], tails[:, :]))
            ch(v.tensor_add(dcol[:, 0:1], tails[:, 0:1], acol[:, 0:1]))
    with nc.Block() as b:
        @b.sync
        def _(sy):
            dO.dinc(sy.dma_start(smalls[:, 0:1024], dtf[:, :]))
            dO.dinc(sy.dma_start(smalls[:, 1024:2048], acs[:, :]))
            dO.dinc(sy.dma_start(smalls[:, 2048:2056], acol[:, :]))
            dO.dinc(sy.dma_start(smalls[:, 2056:2064], tails[:, :]))
            dO.dinc(sy.dma_start(smalls[:, 2064:2072], abias[:, :]))
            dO.dinc(sy.dma_start(smalls[:, 2072:2073], dcol[:, :], allow_slow_non_contiguous=True))
            dO.dinc(sy.dma_start(smalls[:, 2080:3104], afull[:, :]))

    if DBG:
        with nc.Block() as b:
            @b.sync
            def _(sy):
                dO.dinc(sy.dma_start(dtrawd, dtraw[:, :]))
    halo_pv = ctx.sb("halo_pv", [128, 240], BF16)
    halo_pv2 = ctx.sb("halo_pv2", [128, 240], BF16)
    cvbE = ctx.sb("cvbE", [128, 1], F32)
    hfix = ctx.sb("hfix", [4, 8192], BF16)
    fixpad = ctx.sb("fixpad", [128, 128], BF16)
    ext6 = ctx.sb("ext6", [128, 6], BF16)
    f3a = ctx.sb("f3a", [128, 3], BF16)
    f3b = ctx.sb("f3b", [128, 3], BF16)
    with nc.Block() as b:
        @b.sync
        def _(sy):
            dO.dinc(sy.dma_start(cc1i, halo_sb[:, :]))
            sy.wait_ge(dO.h, dO.v)
    with nc.Block() as b:
        @b.gpsimd
        def _(gp):
            sC.inc(gp.collective_compute(
                "AllGather", ALU.bypass, GROUPS, [cc1i.opt()], [cc1g.opt()]))
            gp.wait_ge(sC.h, sC.v)
            with gp.register("rk") as rk, gp.register("off") as off:
                gp.cc_rank_ld(rk, replica_groups=GROUPS)
                gp.reg_mov(off, 0)
                with gp.If_eq(rk, 0):
                    gp.reg_mov(off, 0)
                with gp.Else():
                    gp.reg_sub(off, rk, 1)
                    gp.reg_mul(off, off, 128 * 240)
                dA.dinc(gp.dma_start(
                    halo_pv[:, :], bass.AP(cc1g.tensor, off, [[240, 128], [1, 240]])))
            gp.wait_ge(dA.h, dA.v)
            gp.memset(fixpad[:, :], 0.0)
    with nc.Block() as b:
        @b.vector
        def _(v):
            v.tensor_scalar(halo_pv2[:, :], halo_pv[:, :], hsel[:, 0:1], None, ALU.mult)
    with ExitStack() as pesE:
        pfx = pesE.enter_context(nc.psum_tensor("pfx", [128, 128], F32))
        for cf in range(80):
            with nc.Block() as b:
                @b.vector
                def _(v, cf=cf):
                    ch = EChain(v, ctx.sem("chV"))
                    ch(v.tensor_copy(ext6[:, 0:3], halo_pv2[:, cf * 3:cf * 3 + 3]))
                    ch(v.tensor_copy(ext6[:, 3:6], own3[:, cf * 3:cf * 3 + 3]))
                    ch(v.tensor_scalar(f3a[:, :], ext6[:, 0:3], cvw[:, cf * 4:cf * 4 + 1], None, ALU.mult))
                    ch(v.scalar_tensor_tensor(f3b[:, :], ext6[:, 1:4], cvw[:, cf * 4 + 1:cf * 4 + 2],
                                              f3a[:, :], ALU.mult, ALU.add))
                    ch(v.scalar_tensor_tensor(f3a[:, :], ext6[:, 2:5], cvw[:, cf * 4 + 2:cf * 4 + 3],
                                              f3b[:, :], ALU.mult, ALU.add))
                    ch(v.scalar_tensor_tensor(f3b[:, :], ext6[:, 3:6], cvw[:, cf * 4 + 3:cf * 4 + 4],
                                              f3a[:, :], ALU.mult, ALU.add))
                    ch(v.tensor_copy(cvbE[:, :], cvb[:, cf:cf + 1]))
            with nc.Block() as b:
                @b.scalar
                def _(s, cf=cf):
                    s.activation(f3a[:, :], f3b[:, :], AF.Silu, bias=cvbE[:, 0:1])
            if cf < 64:
                with nc.Block() as b:
                    @b.vector
                    def _(v, cf=cf):
                        v.tensor_copy(fixpad[:, 0:3], f3a[:, :])
                with nc.Block() as b:
                    @b.tensor
                    def _(t, cf=cf):
                        t.matmul(pfx[:, :], fixpad[:, :], cm["identb"][:, :], start=True, stop=True)
                with nc.Block() as b:
                    @b.scalar
                    def _(s, cf=cf):
                        s.copy(hfix[0:3, cf * 128:(cf + 1) * 128], pfx[0:3, :])
            else:
                with nc.Block() as b:
                    @b.vector
                    def _(v, cf=cf):
                        v.tensor_copy(bcfm[cf - 64][:, 0:3], f3a[:, :])
    with nc.Block() as b:
        @b.sync
        def _(sy):
            dO.dinc(sy.dma_start(hfixd, hfix[:, :]))
            for i in range(16):
                dO.dinc(sy.dma_start(bcd[i], bcfm[i][:, :]))
            sy.wait_ge(dO.h, dO.v)
            sy.wait_ge(dA.h, dA.v)

        @b.gpsimd
        def _(gp):
            gp.wait_ge(dW.h, dW.v)
    pesD.close()
    des.close()
    ctx.end_group()
    return ctx




def build_g4(ctx=None):
    ctx = ctx if ctx is not None else Ctx("g4")
    ctx.begin_group("g4")
    nc = ctx.nc
    hidd = ctx.din("hidd", [8, 64, 128, 128], BF16)
    bcd = ctx.din("bcd", [16, 128, 1024], BF16)
    smalls = ctx.din("smalls", [128, 3104], F32)
    hfixd = ctx.din("hfixd", [4, 8192], BF16)
    selT_i = ctx.din("selT", [4, 128])
    mb_i = ctx.din("maskbit", [4, 1])
    pind = ctx.dout("pind", [128, 8192], BF16)
    cc2i = ctx.dint("cc2i", [129, 8192], F32)
    cc2g = ctx.dint("cc2g", [516, 8192], F32)

    cm = setup_common(ctx)
    dA = ctx.sem("dA")
    dO = ctx.sem("dO")
    sC = ctx.sem("sC")
    sP = ctx.sem("sP")
    sV = ctx.sem("sV")

    dtf = ctx.sb("dtf", [128, 1024], F32)
    acs = ctx.sb("acs", [128, 1024], F32)
    abias = ctx.sb("abias", [128, 8], F32)
    dcol = ctx.sb("dcol", [128, 1], F32)
    ab0 = ctx.sb("ab0", [128, 1], F32)
    hfix = ctx.sb("hfix", [4, 8192], BF16)
    Bfm = [ctx.sb(f"Bfm{i}", [128, 1024], BF16) for i in range(8)]
    sT = ctx.sb("sT", [4, 128], F32)
    mbc = ctx.sb("mbc", [4, 1], F32)
    identf = ctx.sb("identf", [128, 128], F32)
    drow = ctx.sb("drow", [1, 128], F32)

    hmark = {}
    with nc.Block() as b:
        @b.sync
        def _(sy):
            dA.dinc(sy.dma_start(dtf[:, :], smalls[:, 0:1024]))
            dA.dinc(sy.dma_start(acs[:, :], smalls[:, 1024:2048]))
            dA.dinc(sy.dma_start(abias[:, :], smalls[:, 2064:2072]))
            dA.dinc(sy.dma_start(dcol[:, :], smalls[:, 2072:2073], allow_slow_non_contiguous=True))
            dA.dinc(sy.dma_start(hfix[:, :], hfixd))
            for g in range(8):
                dA.dinc(sy.dma_start(Bfm[g][:, :], bcd[g]))
            dA.dinc(sy.dma_start(sT[:, :], selT_i))
            dA.dinc(sy.dma_start(mbc[:, :], mb_i))

        @b.vector
        def _(v):
            v.tensor_scalar(identf[:, :], cm["iot"][:, :], cm["pidx"][:, 0:1], None, ALU.is_equal)
    cons_ready = dA.v

    p1es = ExitStack()
    hid_tm = [ctx.sbx(p1es, f"hidtm{i}", [128, 8192], BF16) for i in range(2)]
    hc1 = ctx.sbx(p1es, "hc1", [128, 8192], BF16)
    Lk = [ctx.sbx(p1es, f"Lk{i}", [128, 8192], F32) for i in range(2)]
    det = ctx.sbx(p1es, "det", [128, 128], F32)
    C1 = ctx.sbx(p1es, "C1", [128, 128], BF16)
    C1T = ctx.sbx(p1es, "C1T", [128, 128], BF16)
    Btm = ctx.sbx(p1es, "Btm", [128, 1024], BF16)

    with nc.Block() as b:
        @b.sync
        def _(sy):
            sy.wait_ge(dA.h, cons_ready)
            hmark[0] = dA.dinc(sy.dma_start(
                hid_tm[0][:, :],
                bass.AP(hidd.tensor, 0, [[128, 8192], [1, 128]]), transpose=True))

        @b.gpsimd
        def _(gp):
            gp.memset(Lk[0][:, :], 0.0)

    pesP = ExitStack()
    ptB = pesP.enter_context(nc.psum_tensor("ptB", [128, 1024], BF16))
    ptC = pesP.enter_context(nc.psum_tensor("ptC", [128, 128], BF16))
    psS = [pesP.enter_context(nc.psum_tensor(f"psS{i}", [128, 1024], F32)) for i in range(2)]

    for c in range(8):
        with nc.Block() as b:
            @b.sync
            def _(sy, c=c):
                if c + 1 < 8:
                    hmark[c + 1] = dA.dinc(sy.dma_start(
                        hid_tm[(c + 1) % 2][:, :],
                        bass.AP(hidd.tensor, (c + 1) * 64 * 128 * 128, [[128, 8192], [1, 128]]),
                        transpose=True))

            @b.vector
            def _(v, c=c):
                ch = EChain(v, ctx.sem("chV"))
                ch(v.tensor_copy(ab0[:, :], abias[:, c:c + 1]))
                if c == 0:
                    v.wait_ge(dA.h, hmark[0])
                    v.tensor_copy(hid_tm[0][0:3, :], hfix[0:3, :])
        with nc.Block() as b:
            @b.scalar
            def _(s, c=c):
                s.activation(det[:, :], acs[:, c * 128:(c + 1) * 128], AF.Exp,
                             bias=ab0[:, 0:1], scale=-1.0)
        with nc.Block() as b:
            @b.vector
            def _(v, c=c):
                v.tensor_tensor(C1[:, :], dtf[:, c * 128:(c + 1) * 128], det[:, :], ALU.mult)
        with nc.Block() as b:
            @b.tensor
            def _(t, c=c):
                t.transpose(ptC[:, :], C1[:, :], cm["identb"][:, :])
                for g in range(8):
                    t.transpose(ptB[:, g * 128:(g + 1) * 128],
                                Bfm[g][:, c * 128:(c + 1) * 128], cm["identb"][:, :])
        with nc.Block() as b:
            @b.vector
            def _(v, c=c):
                v.tensor_copy(C1T[:, :], ptC[:, :])
                v.tensor_copy(Btm[:, :], ptB[:, :])
        with nc.Block() as b:
            @b.gpsimd
            def _(gp, c=c):
                gp.wait_ge(dA.h, hmark[c])
                bc = bass.AP(C1T, 0, [list(C1T[:, :].ap[0]), [1, 128], [0, 64]])
                gp.tensor_tensor(
                    hc1[:, :].rearrange("p (h q) -> p h q", q=64),
                    hid_tm[c % 2][:, :].rearrange("p (h q) -> p h q", q=64),
                    bc, ALU.mult)
        pm0 = sP.v
        pv0 = sV.v
        with nc.Block() as b:
            @b.tensor
            def _(t, c=c, pm0=pm0, pv0=pv0):
                for g in range(8):
                    if g >= 2:
                        t.wait_ge(sV.h, pv0 + g - 1)
                    for hh in range(16):
                        ins = t.matmul(psS[g % 2][:, hh * 64:(hh + 1) * 64],
                                       Btm[:, g * 128:(g + 1) * 128],
                                       hc1[:, (g * 16 + hh) * 64:(g * 16 + hh + 1) * 64],
                                       start=True, stop=True)
                        if hh == 15:
                            sP.inc(ins)

            @b.vector
            def _(v, c=c, pm0=pm0, pv0=pv0):
                for g in range(8):
                    v.wait_ge(sP.h, pm0 + g + 1)
                    sV.inc(v.scalar_tensor_tensor(
                        Lk[(c + 1) % 2][:, g * 1024:(g + 1) * 1024],
                        psS[g % 2][:, :], 1.0,
                        Lk[c % 2][:, g * 1024:(g + 1) * 1024],
                        ALU.mult, ALU.add))

    padf = ctx.sb("padf", [128, 128], F32)
    with nc.Block() as b:
        @b.gpsimd
        def _(gp):
            gp.memset(padf[:, :], 0.0)
    with nc.Block() as b:
        @b.vector
        def _(v):
            v.tensor_copy(padf[:, 0:1], dcol[:, :])
    with nc.Block() as b:
        @b.tensor
        def _(t):
            t.matmul(psS[0][:, 0:128], padf[:, :], identf[:, :], start=True, stop=True)
    with nc.Block() as b:
        @b.scalar
        def _(s):
            s.copy(drow[:, :], psS[0][0:1, 0:128])
    with nc.Block() as b:
        @b.sync
        def _(sy):
            dO.dinc(sy.dma_start(cc2i[0:128, :], Lk[0][:, :]))
            dO.dinc(sy.dma_start(cc2i[128:129, 0:128], drow[:, :]))
            sy.wait_ge(dO.h, dO.v)
    pesP.close()
    p1es.close()

    ces = ExitStack()
    Lkg = ctx.sbx(ces, "Lkg", [128, 8192], F32)
    scr = ctx.sbx(ces, "scr", [128, 8192], F32)
    P0 = [ctx.sbx(ces, f"P0{i}", [128, 8192], F32) for i in range(2)]
    d4 = ctx.sbx(ces, "d4", [4, 128], F32)
    coefm = ctx.sbx(ces, "coefm", [4, 128], F32)
    coefm2 = ctx.sbx(ces, "coefm2", [4, 128], F32)
    crow = ctx.sbx(ces, "crow", [1, 128], BF16)
    coefb = [ctx.sbx(ces, f"coefb{i}", [128, 128], BF16) for i in range(4)]
    pinb = ctx.sbx(ces, "pinb", [128, 8192], BF16)
    with nc.Block() as b:
        @b.gpsimd
        def _(gp):
            sC.inc(gp.collective_compute(
                "AllGather", ALU.bypass, GROUPS, [cc2i.opt()], [cc2g.opt()]))
            gp.wait_ge(sC.h, sC.v)
            for j in range(4):
                dA.dinc(gp.dma_start(d4[j:j + 1, :], cc2g[j * 129 + 128:j * 129 + 129, 0:128]))
            gp.wait_ge(dA.h, dA.v)
            gp.memset(P0[0][:, :], 0.0)
    with ExitStack() as pes2:
        pcb = pes2.enter_context(nc.psum_tensor("pcb", [128, 128], F32))
        with nc.Block() as b:
            @b.tensor
            def _(t):
                t.matmul(pcb[:, :], sT[0:4, :], d4[0:4, :], start=True, stop=True)
        with nc.Block() as b:
            @b.scalar
            def _(s):
                s.activation(coefm[:, :], pcb[0:4, :], AF.Exp)
        with nc.Block() as b:
            @b.vector
            def _(v):
                v.tensor_scalar(coefm2[:, :], coefm[:, :], mbc[:, 0:1], None, ALU.mult)
        for m in range(4):
            with nc.Block() as b:
                @b.gpsimd
                def _(gp, m=m):
                    dA.dinc(gp.dma_start(crow[:, :], coefm2[m:m + 1, :]))
                    gp.wait_ge(dA.h, dA.v)
            with nc.Block() as b:
                @b.tensor
                def _(t, m=m):
                    t.matmul(pcb[:, :], cm["onescol"][0:1, :], crow[0:1, :], start=True, stop=True)
            with nc.Block() as b:
                @b.vector
                def _(v, m=m):
                    v.tensor_copy(coefb[m][:, :], pcb[:, :])
    lgm = {}
    for m in range(4):
        with nc.Block() as b:
            @b.sync
            def _(sy, m=m):
                lgm[m] = dA.dinc(sy.dma_start(Lkg[:, :], cc2g[m * 129:m * 129 + 128, :]))
        with nc.Block() as b:
            @b.vector
            def _(v, m=m):
                v.wait_ge(dA.h, lgm[m])
                bc = bass.AP(coefb[m], 0, [list(coefb[m][:, :].ap[0]), [1, 128], [0, 64]])
                v.tensor_tensor(scr[:, :].rearrange("p (h q) -> p h q", q=64),
                                Lkg[:, :].rearrange("p (h q) -> p h q", q=64), bc, ALU.mult)
        with nc.Block() as b:
            @b.gpsimd
            def _(gp, m=m):
                gp.tensor_add(P0[(m + 1) % 2][:, :], P0[m % 2][:, :], scr[:, :])
    with nc.Block() as b:
        @b.vector
        def _(v):
            v.tensor_copy(pinb[:, :], P0[0][:, :])
    with nc.Block() as b:
        @b.sync
        def _(sy):
            dO.dinc(sy.dma_start(pind, pinb[:, :]))
            sy.wait_ge(dO.h, dO.v)
            sy.wait_ge(dA.h, dA.v)
    ces.close()
    ctx.end_group()
    return ctx




def build_g5(ctx=None):
    ctx = ctx if ctx is not None else Ctx("g5")
    ctx.begin_group("g5")
    nc = ctx.nc
    hidd = ctx.din("hidd", [8, 64, 128, 128], BF16)
    gated = ctx.din("gated", [8, 64, 128, 128], BF16)
    bcd = ctx.din("bcd", [16, 128, 1024], BF16)
    smalls = ctx.din("smalls", [128, 3104], F32)
    hfixd = ctx.din("hfixd", [4, 8192], BF16)
    pind = ctx.din("pind", [128, 8192], BF16)
    dp_row = ctx.din("dp_row", [1, 128])
    gnw_row = ctx.din("gnw_row", [1, 8192])
    wop_k = ctx.din("wop_k", [64, 32, 128, 128])
    mamd = ctx.dout("mamd", [4096, 1024], F32)
    if DBG:
        ygd = ctx.dout("ygd", [8, 128, 8192], BF16)
        ltd = ctx.dout("ltd", [128, 8192], BF16)
        mmd = ctx.dout("mmd", [128, 8192], BF16)
        d0d = ctx.dout("d0d", [128, 8192], BF16)
        ynd = ctx.dout("ynd", [8, 128, 8192], BF16)
        pdbg = ctx.dout("pdbg", [8, 128, 8192], BF16)
    ead = ctx.dint("ead", [128, 128], F32)
    yfmd = ctx.dint("yfmd", [8, 64, 128, 128], BF16)

    cm = setup_common(ctx)
    dA = ctx.sem("dA")
    dG = ctx.sem("dG")
    dO = ctx.sem("dO")
    sP = ctx.sem("sP")
    sV = ctx.sem("sV")
    sA = ctx.sem("sA")
    sG = ctx.sem("sG")

    dtf = ctx.sb("dtf", [128, 1024], F32)
    acs = ctx.sb("acs", [128, 1024], F32)
    afull = ctx.sb("afull", [128, 1024], F32)
    acol = ctx.sb("acol", [128, 8], F32)
    ab0 = ctx.sb("ab0", [128, 1], F32)
    dprb = ctx.sb("dprb", [1, 128], BF16)
    ges = ExitStack()
    gnwr = ctx.sbx(ges, "gnwr", [1, 8192], BF16)
    Dpb = ctx.sb("Dpb", [128, 128], BF16)
    gnwb = ctx.sb("gnwb", [128, 8192], BF16)
    mlt = ctx.sb("mlt", [128, 128], BF16)
    eps128 = cm["eps128"]

    with nc.Block() as b:
        @b.vector
        def _(v):
            v.tensor_scalar(mlt[:, :], cm["iot"][:, :], cm["pidx"][:, 0:1], None, ALU.is_gt)

        @b.sync
        def _(sy):
            dA.dinc(sy.dma_start(dtf[:, :], smalls[:, 0:1024]))
            dA.dinc(sy.dma_start(acs[:, :], smalls[:, 1024:2048]))
            dA.dinc(sy.dma_start(acol[:, :], smalls[:, 2048:2056]))
            dA.dinc(sy.dma_start(afull[:, :], smalls[:, 2080:3104]))

        @b.gpsimd
        def _(gp):
            dA.dinc(gp.dma_start(dprb[:, :], dp_row))
            dA.dinc(gp.dma_start(gnwr[:, :], gnw_row))
    setup_ready = dA.v
    with ExitStack() as pes:
        pbq = pes.enter_context(nc.psum_tensor("pbqG", [128, 512], F32))
        with nc.Block() as b:
            @b.tensor
            def _(t):
                t.wait_ge(dA.h, setup_ready)
                t.matmul(pbq[:, 0:128], cm["onescol"][0:1, :], dprb[0:1, :], start=True, stop=True)
        with nc.Block() as b:
            @b.vector
            def _(v):
                v.tensor_copy(Dpb[:, :], pbq[:, 0:128])
        for h in range(16):
            with nc.Block() as b:
                @b.tensor
                def _(t, h=h):
                    t.matmul(pbq[:, :], cm["onescol"][0:1, :], gnwr[0:1, h * 512:(h + 1) * 512],
                             start=True, stop=True)
            with nc.Block() as b:
                @b.vector
                def _(v, h=h):
                    v.tensor_copy(gnwb[:, h * 512:(h + 1) * 512], pbq[:, :])
    ges.close()

    mes = ExitStack()
    hid_tm = ctx.sbx(mes, "hid_tm", [128, 8192], BF16)
    hc_st = ctx.sbx(mes, "hc_st", [128, 8192], BF16)
    P = ctx.sbx(mes, "P", [128, 8192], BF16)
    scr512 = ctx.sbx(mes, "scr512", [128, 512], BF16)
    d0h = ctx.sbx(mes, "d0h", [128, 8192], BF16)
    Lth = ctx.sbx(mes, "Lth", [128, 8192], BF16)
    d1h = ctx.sbx(mes, "d1h", [128, 8192], BF16)
    YG = ctx.sbx(mes, "YG", [128, 8192], BF16)
    yfm = ctx.sbx(mes, "yfm", [128, 8192], BF16)
    Bc = ctx.sbx(mes, "Bc", [128, 1024], BF16)
    Cc = ctx.sbx(mes, "Cc", [128, 1024], BF16)
    Btm = ctx.sbx(mes, "Btm", [128, 1024], BF16)
    Msb = ctx.sbx(mes, "Msb", [128, 1024], BF16)
    det = ctx.sbx(mes, "det", [128, 128], F32)
    C2 = ctx.sbx(mes, "C2", [128, 128], BF16)
    C2T = ctx.sbx(mes, "C2T", [128, 128], BF16)
    dtT = ctx.sbx(mes, "dtT", [128, 128], BF16)
    SbT = ctx.sbx(mes, "SbT", [128, 128], BF16)
    eacs = ctx.sbx(mes, "eacs", [128, 128], F32)
    easb = ctx.sbx(mes, "easb", [128, 128], F32)
    colb = ctx.sbx(mes, "colb", [128, 128], BF16)
    eacb = ctx.sbx(mes, "eacb", [128, 128], BF16)
    eacc = ctx.sbx(mes, "eacc", [128, 1], F32)
    earow = ctx.sbx(mes, "earow", [1, 2048], BF16)
    gbuf = [ctx.sbx(mes, f"gbuf{i}", [128, 512], BF16) for i in range(16)]
    yoffsb = [ctx.sbx(mes, f"yoffsb{i}", [128, 512], BF16) for i in range(2)]
    t5 = ctx.sbx(mes, "t5", [128, 512], BF16)
    hdp = ctx.sbx(mes, "hdp", [128, 512], BF16)
    t2g = ctx.sbx(mes, "t2g", [128, 512], BF16)
    ssqc = [ctx.sbx(mes, f"ssqc{i}", [128, 1], F32) for i in range(8)]
    ssq8 = ctx.sbx(mes, "ssq8", [128, 8], F32)
    rstd8 = ctx.sbx(mes, "rstd8", [128, 8], F32)
    rsc = ctx.sbx(mes, "rsc", [128, 8], F32)


    hmark = {}
    gmark = {}
    bmark = {}
    eamark = {}
    sg_abs = {}

    def gload(sy, c, hg):
        gmark[(c, hg)] = dG.dinc(sy.dma_start(
            gbuf[hg][:, :],
            bass.AP(gated.tensor, c * 64 * 128 * 128 + hg * 4 * 128 * 128,
                    [[128, 512], [1, 128]]), transpose=True))

    for c in range(8):
        with nc.Block() as b:
            @b.sync
            def _(sy, c=c):
                hmark[c] = dA.dinc(sy.dma_start(
                    hid_tm[:, :],
                    bass.AP(hidd.tensor, c * 64 * 128 * 128, [[128, 8192], [1, 128]]),
                    transpose=True))
                if c == 0:
                    sy.wait_ge(dA.h, hmark[0])
                    hmark[0] = dA.dinc(sy.dma_start(
                        hid_tm[0:3, :], bass.AP(hfixd.tensor, 0, [[8192, 3], [1, 8192]])))
                for g in range(16):
                    bmark[(c, g)] = dA.dinc(sy.dma_start(
                        (Bc if g < 8 else Cc)[:, (g % 8) * 128:(g % 8 + 1) * 128],
                        bcd[g][:, c * 128:(c + 1) * 128]))
                for hg in range(16):
                    idx = (c - 1) * 16 + hg
                    if idx >= 0:
                        sy.wait_ge(sG.h, sg_abs[idx])
                    gload(sy, c, hg)

            @b.vector
            def _(v, c=c):
                v.tensor_copy(ab0[:, :], acol[:, c:c + 1])
        with nc.Block() as b:
            @b.scalar
            def _(s, c=c):
                ch = EChain(s, ctx.sem("chS"))
                ch(s.activation(det[:, :], acs[:, c * 128:(c + 1) * 128], AF.Exp,
                                bias=ab0[:, 0:1], scale=-1.0))
                ch(s.activation(eacs[:, :], acs[:, c * 128:(c + 1) * 128], AF.Exp))
                ch(s.activation(easb[:, :], afull[:, c * 128:(c + 1) * 128], AF.Exp))
                ch(s.activation(eacc[:, :], ab0[:, 0:1], AF.Exp))

        with nc.Block() as b:
            @b.vector
            def _(v, c=c):
                ch = EChain(v, ctx.sem("chV"))
                v.tensor_tensor(C2[:, :], dtf[:, c * 128:(c + 1) * 128], det[:, :], ALU.mult)
                ch(v.tensor_scalar(colb[:, :], cm["ones_bf"][:, :], eacc[:, 0:1], None, ALU.mult))

            @b.sync
            def _(sy, c=c):
                eamark[c] = dO.dinc(sy.dma_start(ead, easb[:, :]))
        with ExitStack() as pesT:
            ptX = pesT.enter_context(nc.psum_tensor(ctx.uname("ptX"), [128, 1024], BF16))
            psM = pesT.enter_context(nc.psum_tensor(ctx.uname("psM"), [128, 1024], F32))
            with nc.Block() as b:
                @b.tensor
                def _(t, c=c):
                    t.wait_ge(dA.h, bmark[(c, 15)])
                    t.transpose(ptX[:, 0:128], C2[:, :], cm["identb"][:, :])
                    t.transpose(ptX[:, 128:256], colb[:, :], cm["identb"][:, :])
                    for g in range(8):
                        t.matmul(psM[:, g * 128:(g + 1) * 128], Bc[:, g * 128:(g + 1) * 128],
                                 Cc[:, g * 128:(g + 1) * 128], start=True, stop=True)
            with nc.Block() as b:
                @b.vector
                def _(v, c=c):
                    v.tensor_copy(C2T[:, :], ptX[:, 0:128])
                    v.tensor_copy(eacb[:, :], ptX[:, 128:256])
                    v.tensor_copy(Msb[:, :], psM[:, :])
            with nc.Block() as b:
                @b.vector
                def _(v, c=c):
                    v.tensor_copy(C2[:, :], dtf[:, c * 128:(c + 1) * 128])
                    v.tensor_copy(colb[:, :], eacs[:, :])
            with nc.Block() as b:
                @b.tensor
                def _(t, c=c):
                    t.transpose(ptX[:, 0:128], C2[:, :], cm["identb"][:, :])
                    t.transpose(ptX[:, 128:256], colb[:, :], cm["identb"][:, :])
            with nc.Block() as b:
                @b.vector
                def _(v, c=c):
                    v.tensor_copy(dtT[:, :], ptX[:, 0:128])
                    v.tensor_copy(SbT[:, :], ptX[:, 128:256])
            with nc.Block() as b:
                @b.tensor
                def _(t, c=c):
                    for g in range(8):
                        t.transpose(ptX[:, g * 128:(g + 1) * 128],
                                    Bc[:, g * 128:(g + 1) * 128], cm["identb"][:, :])
            with nc.Block() as b:
                @b.vector
                def _(v, c=c):
                    v.tensor_copy(Btm[:, :], ptX[:, :])
        if c == 0:
            with nc.Block() as b:
                @b.sync
                def _(sy):
                    pm = dA.dinc(sy.dma_start(P[:, :], pind))
                    sy.wait_ge(dA.h, pm)
        with nc.Block() as b:
            @b.gpsimd
            def _(gp, c=c):
                gp.wait_ge(dA.h, hmark[c])
                bc2 = bass.AP(C2T, 0, [list(C2T[:, :].ap[0]), [1, 128], [0, 64]])
                gp.tensor_tensor(hc_st[:, :].rearrange("p (h q) -> p h q", q=64),
                                 hid_tm[:, :].rearrange("p (h q) -> p h q", q=64), bc2, ALU.mult)
        for half in range(2):
            with ExitStack() as pesH:
                pd0 = [pesH.enter_context(nc.psum_tensor(ctx.uname("pd0"), [128, 512], F32))
                       for _ in range(2)]
                for qq in range(4):
                    with nc.Block(no_gpsimd_drain=True) as b:
                        @b.gpsimd
                        def _(gp, c=c, half=half, qq=qq):
                            if qq == 0:
                                gp.wait_ge(dO.h, eamark[c])
                            em = dA.dinc(gp.dma_start(
                                earow[:, :], bass.AP(ead.tensor, half * 8192 + qq * 2048,
                                                     [[1, 1], [1, 2048]])))
                            gp.wait_ge(dA.h, em)
                    d0m0 = sP.v
                    d0v = sV.v
                    d0a = sA.v
                    with nc.Block() as b:
                        @b.tensor
                        def _(t, c=c, half=half, qq=qq, d0m0=d0m0, d0v=d0v, d0a=d0a):
                            for q in range(4):
                                if q >= 2:
                                    if q % 2 == 0:
                                        t.wait_ge(sV.h, d0v + q // 2)
                                    else:
                                        t.wait_ge(sA.h, d0a + q // 2)
                                sP.inc(t.matmul(pd0[q % 2][:, :], cm["onescol"][0:1, :],
                                                earow[0:1, q * 512:(q + 1) * 512],
                                                start=True, stop=True))

                        @b.vector
                        def _(v, c=c, half=half, qq=qq, d0m0=d0m0):
                            ml_ap = bass.AP(mlt, 0, [list(mlt[:, :].ap[0]), [0, 4], [1, 128]])
                            for q in range(4):
                                v.wait_ge(sP.h, d0m0 + q + 1)
                                qa = qq * 4 + q
                                ins = v.tensor_tensor(
                                    d0h[:, qa * 512:(qa + 1) * 512].rearrange("p (a w) -> p a w", w=128),
                                    pd0[q % 2][:, :].rearrange("p (a w) -> p a w", w=128),
                                    ml_ap, ALU.mult)
                                if q % 2 == 0:
                                    sV.inc(ins)
                                else:
                                    sA.inc(ins)
                with nc.Block() as b:
                    @b.gpsimd
                    def _(gp, c=c, half=half):
                        id_ap = bass.AP(cm["identb"], 0,
                                        [list(cm["identb"][:, :].ap[0]), [0, 64], [1, 128]])
                        dt_ap = bass.AP(dtT, half * 64, [list(dtT[:, :].ap[0]), [1, 64], [0, 128]])
                        gp.tensor_tensor(d1h[:, :].rearrange("p (a c2) -> p a c2", c2=128),
                                         id_ap, dt_ap, ALU.mult)
                if DBG and c == 0 and half == 0:
                    with nc.Block() as b:
                        @b.sync
                        def _(sy):
                            dO.dinc(sy.dma_start(d0d, d0h[:, :]))
                with nc.Block() as b:
                    @b.vector
                    def _(v, c=c, half=half):
                        v.tensor_tensor_scan(Lth[:, :], d0h[:, :], d1h[:, :], 0.0, ALU.mult, ALU.add)
                if DBG and c == 0 and half == 0:
                    with nc.Block() as b:
                        @b.sync
                        def _(sy):
                            dO.dinc(sy.dma_start(ltd, Lth[:, :]))
                with nc.Block() as b:
                    @b.vector
                    def _(v, c=c, half=half):
                        mb = bass.AP(Msb, Msb[:, half * 512:half * 512 + 1].offset,
                                     [list(Msb[:, :].ap[0]), [128, 4], [0, 16], [1, 128]])
                        v.tensor_tensor(d0h[:, :].rearrange("p (g r i) -> p g r i", r=16, i=128),
                                        Lth[:, :].rearrange("p (g r i) -> p g r i", r=16, i=128),
                                        mb, ALU.mult)
                if DBG and c == 0 and half == 0:
                    with nc.Block() as b:
                        @b.sync
                        def _(sy):
                            dO.dinc(sy.dma_start(mmd, d0h[:, :]))
                with ExitStack() as pesY:
                    psY = [pesY.enter_context(nc.psum_tensor(ctx.uname("psY"), [128, 512], F32))
                           for _ in range(2)]
                    psO = [pesY.enter_context(nc.psum_tensor(ctx.uname("psO"), [128, 512], F32))
                           for _ in range(2)]
                    psS = [pesY.enter_context(nc.psum_tensor(ctx.uname("psS"), [128, 512], F32))
                           for _ in range(2)]
                    y0p = sP.v
                    y0v = sV.v
                    y0a = sA.v
                    with nc.Block() as b:
                        @b.tensor
                        def _(t, c=c, half=half, y0p=y0p, y0v=y0v, y0a=y0a):
                            for q in range(8):
                                if q >= 2:
                                    t.wait_ge(sV.h, y0v + (q - 1) * 2)
                                    t.wait_ge(sA.h, y0a + q - 1)
                                for hh in range(8):
                                    hl = q * 8 + hh
                                    hgl = half * 64 + hl
                                    t.matmul(psY[q % 2][:, hh * 64:(hh + 1) * 64],
                                             d0h[:, hl * 128:(hl + 1) * 128],
                                             hid_tm[:, hgl * 64:(hgl + 1) * 64],
                                             start=True, stop=True)
                                for hh in range(8):
                                    hgl = half * 64 + q * 8 + hh
                                    g = hgl // 16
                                    t.matmul(psO[q % 2][:, hh * 64:(hh + 1) * 64],
                                             Cc[:, g * 128:(g + 1) * 128],
                                             P[:, hgl * 64:(hgl + 1) * 64],
                                             start=True, stop=True)
                                ins = None
                                for hh in range(8):
                                    hgl = half * 64 + q * 8 + hh
                                    g = hgl // 16
                                    ins = t.matmul(psS[q % 2][:, hh * 64:(hh + 1) * 64],
                                                   Btm[:, g * 128:(g + 1) * 128],
                                                   hc_st[:, hgl * 64:(hgl + 1) * 64],
                                                   start=True, stop=True)
                                sP.inc(ins)

                        @b.scalar
                        def _(s, c=c, half=half, y0p=y0p, y0v=y0v):
                            for q in range(8):
                                s.wait_ge(sP.h, y0p + q + 1)
                                if q >= 2:
                                    s.wait_ge(sV.h, y0v + (q - 1) * 2)
                                sA.inc(s.copy(yoffsb[q % 2][:, :], psO[q % 2][:, :]))

                        @b.vector
                        def _(v, c=c, half=half, y0p=y0p, y0a=y0a):
                            ch = EChain(v, ctx.sem("chV"))
                            for q in range(8):
                                hg = half * 8 + q
                                v.wait_ge(sA.h, y0a + q + 1)
                                sb_ap = bass.AP(SbT, SbT[:, hg * 8:hg * 8 + 1].offset,
                                                [list(SbT[:, :].ap[0]), [1, 8], [0, 64]])
                                ch(v.tensor_tensor(t5[:, :].rearrange("p (h w) -> p h w", w=64),
                                                   yoffsb[q % 2][:, :].rearrange("p (h w) -> p h w", w=64),
                                                   sb_ap, ALU.mult))
                                v.wait_ge(sP.h, y0p + q + 1)
                                ch(v.scalar_tensor_tensor(YG[:, hg * 512:(hg + 1) * 512],
                                                          psY[q % 2][:, :], 1.0, t5[:, :],
                                                          ALU.mult, ALU.add))
                                ec_ap = bass.AP(eacb, hg * 8,
                                                [list(eacb[:, :].ap[0]), [1, 8], [0, 64]])
                                ch(v.tensor_tensor(scr512[:, :].rearrange("p (h w) -> p h w", w=64),
                                                   P[:, hg * 512:(hg + 1) * 512].rearrange(
                                                       "p (h w) -> p h w", w=64),
                                                   ec_ap, ALU.mult))
                                sV.inc(v.scalar_tensor_tensor(P[:, hg * 512:(hg + 1) * 512],
                                                              psS[q % 2][:, :], 1.0,
                                                              scr512[:, :],
                                                              ALU.mult, ALU.add))
                                sV.inc(v.engine_nop())

                        @b.gpsimd
                        def _(gp, c=c, half=half, y0v=y0v):
                            chg = EChain(gp, ctx.sem("chG"))
                            for q in range(8):
                                hg = half * 8 + q
                                gp.wait_ge(sV.h, y0v + q * 2 + 2)
                                dp_ap = bass.AP(Dpb, hg * 8,
                                                [list(Dpb[:, :].ap[0]), [1, 8], [0, 64]])
                                chg(gp.tensor_tensor(hdp[:, :].rearrange("p (h w) -> p h w", w=64),
                                                     hid_tm[:, hg * 512:(hg + 1) * 512].rearrange(
                                                         "p (h w) -> p h w", w=64),
                                                     dp_ap, ALU.mult))
                                chg(gp.tensor_add(t2g[:, :], YG[:, hg * 512:(hg + 1) * 512], hdp[:, :]))
                                gp.wait_ge(dG.h, gmark[(c, hg)])
                                chg(gp.tensor_tensor(YG[:, hg * 512:(hg + 1) * 512], t2g[:, :],
                                                     gbuf[hg][:, :], ALU.mult))
                                sg_abs[c * 16 + hg] = sG.inc(gp.engine_nop())

        if DBG:
            with nc.Block() as b:
                @b.sync
                def _(sy, c=c):
                    dO.dinc(sy.dma_start(ygd[c], YG[:, :]))
                    dO.dinc(sy.dma_start(pdbg[c], P[:, :]))
        with nc.Block() as b:
            @b.scalar
            def _(s, c=c):
                ch = EChain(s, ctx.sem("chS"))
                for g in range(8):
                    ch(s.activation(yfm[:, 0:1024], YG[:, g * 1024:(g + 1) * 1024], AF.Square,
                                    accum_out=ssqc[g][:, 0:1]))
        with nc.Block() as b:
            @b.vector
            def _(v, c=c):
                for g in range(8):
                    v.tensor_copy(ssq8[:, g:g + 1], ssqc[g][:, :])
        with nc.Block() as b:
            @b.scalar
            def _(s, c=c):
                ch = EChain(s, ctx.sem("chS"))
                ch(s.activation(rsc[:, :], ssq8[:, :], AF.Ln, bias=eps128[:, 0:1], scale=1.0 / 1024.0))
                ch(s.activation(rstd8[:, :], rsc[:, :], AF.Exp, bias=0.0, scale=-0.5))
        with nc.Block() as b:
            @b.vector
            def _(v, c=c):
                for g in range(8):
                    v.scalar_tensor_tensor(Lth[:, g * 1024:(g + 1) * 1024],
                                           YG[:, g * 1024:(g + 1) * 1024], rstd8[:, g:g + 1],
                                           gnwb[:, g * 1024:(g + 1) * 1024], ALU.mult, ALU.mult)
        if DBG:
            with nc.Block() as b:
                @b.sync
                def _(sy, c=c):
                    dO.dinc(sy.dma_start(ynd[c], Lth[:, :]))
        with ExitStack() as pesT2:
            pty = [pesT2.enter_context(nc.psum_tensor(ctx.uname("pty"), [128, 1024], BF16))
                   for _ in range(2)]
            t0p = sP.v
            t0v = sV.v
            with nc.Block() as b:
                @b.tensor
                def _(t, c=c, t0v=t0v):
                    for g in range(8):
                        if g >= 2:
                            t.wait_ge(sV.h, t0v + g - 1)
                        ins = None
                        for k in range(8):
                            cf = g * 8 + k
                            ins = t.transpose(pty[g % 2][:, k * 128:(k + 1) * 128],
                                              Lth[:, cf * 128:(cf + 1) * 128], cm["identb"][:, :])
                        sP.inc(ins)

                @b.vector
                def _(v, c=c, t0p=t0p):
                    for g in range(8):
                        v.wait_ge(sP.h, t0p + g + 1)
                        sV.inc(v.tensor_copy(yfm[:, g * 1024:(g + 1) * 1024], pty[g % 2][:, :]))
        with nc.Block() as b:
            @b.sync
            def _(sy, c=c):
                dst = bass.AP(yfmd.tensor, c * 64 * 128 * 128,
                              [[128, 128], [128 * 128, 64], [1, 128]])
                dO.dinc(sy.dma_start(dst, yfm[:, :]))
    mes.close()

    oes = ExitStack()
    rhs = [ctx.sbx(oes, f"oprhs{i}", [128, 1024], BF16) for i in range(4)]
    wsl4 = [ctx.sbx(oes, f"wsl4_{i}", [128, 512], BF16) for i in range(2)]
    mbuf = [ctx.sbx(oes, f"mbuf{i}", [128, 1024], F32) for i in range(4)]
    rm = {}
    wm = {}
    yfm_done = dO.v

    def rload(sy, grp, k):
        rm[(grp, k)] = dA.dinc(sy.dma_start(
            rhs[k % 4][:, :],
            bass.AP(yfmd.tensor, k * 128 * 128,
                    [[128, 128], [64 * 128 * 128, 8], [1, 128]])))

    def wload(gp, grp, k):
        wm[(grp, k)] = dA.dinc(gp.dma_start(
            wsl4[k % 2][:, :],
            bass.AP(wop_k.tensor, k * 32 * 128 * 128 + grp * 4 * 128 * 128,
                    [[128, 128], [128 * 128, 4], [1, 128]])))

    with ExitStack() as pesO:
        pmo = [pesO.enter_context(nc.psum_tensor(f"pmo{i}", [128, 1024], F32)) for i in range(4)]
        for grp in range(8):
            with nc.Block() as b:
                @b.sync
                def _(sy, grp=grp):
                    sy.wait_ge(dO.h, yfm_done)
                    for k in range(2):
                        rload(sy, grp, k)

                @b.gpsimd
                def _(gp, grp=grp):
                    wload(gp, grp, 0)
            for kc in range(64):
                with nc.Block(no_gpsimd_drain=True) as b:
                    @b.sync
                    def _(sy, grp=grp, kc=kc):
                        if kc + 2 < 64:
                            rload(sy, grp, kc + 2)

                    @b.gpsimd
                    def _(gp, grp=grp, kc=kc):
                        if kc + 1 < 64:
                            wload(gp, grp, kc + 1)

                    @b.tensor
                    def _(t, grp=grp, kc=kc):
                        t.wait_ge(dA.h, rm[(grp, kc)])
                        t.wait_ge(dA.h, wm[(grp, kc)])
                        for j in range(4):
                            for h in range(2):
                                t.matmul(pmo[j][:, h * 512:(h + 1) * 512],
                                         wsl4[kc % 2][:, j * 128:(j + 1) * 128],
                                         rhs[kc % 4][:, h * 512:(h + 1) * 512],
                                         start=(kc == 0), stop=(kc == 63))
            with nc.Block() as b:
                @b.vector
                def _(v, grp=grp):
                    for j in range(4):
                        v.tensor_copy(mbuf[j][:, :], pmo[j][:, :])
            with nc.Block() as b:
                @b.sync
                def _(sy, grp=grp):
                    for j in range(4):
                        oc = grp * 4 + j
                        dO.dinc(sy.dma_start(mamd[oc * 128:(oc + 1) * 128, :], mbuf[j][:, :]))
    with nc.Block() as b:
        @b.sync
        def _(sy):
            sy.wait_ge(dO.h, dO.v)
            sy.wait_ge(dA.h, dA.v)

        @b.scalar
        def _(s):
            s.wait_ge(dG.h, dG.v)
    oes.close()
    ctx.end_group()
    return ctx




def build_g6(ctx=None):
    ctx = ctx if ctx is not None else Ctx("g6")
    ctx.begin_group("g6")
    nc = ctx.nc
    xn1d = ctx.din("xn1d", [4096, 1024])
    x1d = ctx.din("x1d", [4096, 1024])
    mamd = ctx.din("mamd", [4096, 1024])
    wb1_l = ctx.din("wb1_l", [4096, 4096])
    bb1_c = ctx.din("bb1_c", [128, 32])
    wg1_r = ctx.din("wg1_r", [4096, 9])
    bthr_i = ctx.din("bthr", [1, 1])
    a1_l = ctx.din("a1_l", [128, 4096])
    b1f_l = ctx.din("b1f_l", [4096, 128])
    eexp2 = ctx.din("eexp2", [8, 128])
    outT = ctx.dout("outT", [4096, 1024])

    cm = setup_common(ctx)
    dA = ctx.sem("dA")
    dW = ctx.sem("dW")
    dO = ctx.sem("dO")

    xn1b = [ctx.sb(f"xn1b{i}", [128, 1024], BF16) for i in range(32)]
    wg1b = ctx.sb("wg1b", [128, 288], BF16)
    bb1 = ctx.sb("bb1", [128, 32], F32)
    bthr_c = ctx.sb("bthr_c", [128, 1], F32)
    bthr_r = ctx.sb("bthr_r", [1, 1], BF16)
    eexpb = ctx.sb("eexpb", [8, 128], BF16)
    routef = ctx.sb("routef", [8, 1024], F32)
    rtpad = ctx.sb("rtpad", [128, 128], BF16)
    route_tm = ctx.sb("route_tm", [128, 8], BF16)
    rt_e = ctx.sb("rt_e", [128, 8], F32)
    rt_t = ctx.sb("rt_t", [128, 8], F32)
    rt_u = ctx.sb("rt_u", [128, 8], F32)
    rcol = [ctx.sb(f"rcol{i}", [128, 1], F32) for i in range(5)]
    rfb = ctx.sb("rfb", [8, 1024], BF16)
    wf1b = ctx.sb("wf1b", [128, 1024], BF16)
    a1sl = ctx.sb("a1sl", [128, 4096], BF16)
    hw1b = ctx.sb("hw1b", [128, 1024], BF16)

    xmark = {}
    with nc.Block() as b:
        @b.gpsimd
        def _(gp):
            for i in range(32):
                xmark[i] = dW.dinc(gp.dma_start(xn1b[i][:, :], xn1d[i * 128:(i + 1) * 128, :]))
            dW.dinc(gp.dma_start(wg1b[:, :], bass.AP(wg1_r.tensor, 0, [[9, 128], [128 * 9, 32], [1, 9]])))
            dW.dinc(gp.dma_start(bthr_r[:, :], bthr_i))
            dW.dinc(gp.dma_start(eexpb[:, :], eexp2))
            dW.dinc(gp.dma_start(a1sl[:, :], a1_l[0:128, :]))

        @b.sync
        def _(sy):
            dA.dinc(sy.dma_start(bb1[:, :], bb1_c))
    in_ready = dW.v

    with ExitStack() as pes:
        pbt = pes.enter_context(nc.psum_tensor("pbt", [128, 512], F32))
        with nc.Block() as b:
            @b.tensor
            def _(t):
                t.wait_ge(dW.h, in_ready)
                t.matmul(pbt[:, 0:1], cm["onescol"][0:1, :], bthr_r[0:1, 0:1],
                         start=True, stop=True)
        with nc.Block() as b:
            @b.vector
            def _(v):
                v.tensor_copy(bthr_c[:, :], pbt[:, 0:1])

        plg = pes.enter_context(nc.psum_tensor("plg6", [128, 512], F32))
        prt = pes.enter_context(nc.psum_tensor("prt6", [128, 128], F32))
        with nc.Block() as b:
            @b.gpsimd
            def _(gp):
                gp.memset(rtpad[:, :], 0.0)
        for tt in range(8):
            with nc.Block() as b:
                @b.tensor
                def _(t, tt=tt):
                    for kc in range(32):
                        t.matmul(plg[:, 0:9], xn1b[kc][:, tt * 128:(tt + 1) * 128],
                                 wg1b[:, kc * 9:(kc + 1) * 9], start=(kc == 0), stop=(kc == 31))
            with nc.Block() as b:
                @b.vector
                def _(v, tt=tt):
                    ch = EChain(v, ctx.sem("chV"))
                    ch(v.reduce_max(rcol[0][:, :], plg[:, 0:8], AXX))
                    ch(v.tensor_scalar_mul(rcol[1][:, :], rcol[0][:, :], -1.0))
            with nc.Block() as b:
                @b.scalar
                def _(s, tt=tt):
                    ch = EChain(s, ctx.sem("chS"))
                    ch(s.activation(rt_e[:, :], plg[:, 0:8], AF.Exp, bias=rcol[1][:, 0:1],
                                    accum_out=rcol[2][:, 0:1]))
                    ch(s.activation(rcol[3][:, :], plg[:, 8:9], AF.Sigmoid, bias=bthr_c[:, 0:1]))
            with nc.Block() as b:
                @b.vector
                def _(v, tt=tt):
                    ch = EChain(v, ctx.sem("chV"))
                    ch(v.reciprocal(rcol[4][:, :], rcol[2][:, 0:1]))
                    ch(v.tensor_scalar(rt_t[:, :], rt_e[:, :], rcol[4][:, 0:1], None, ALU.mult))
                    ch(v.tensor_scalar_mul(rcol[0][:, :], rcol[3][:, 0:1], 0.125))
                    ch(v.tensor_scalar(rt_u[:, :], rt_t[:, :], rcol[0][:, 0:1], None, ALU.subtract))
                    ch(v.tensor_relu(rt_t[:, :], rt_u[:, :]))
                    ch(v.reduce_sum(rcol[1][:, :], rt_t[:, :], AXX))
                    ch(v.tensor_single_scalar(rcol[2][:, :], rcol[1][:, 0:1], 0.0, ALU.is_le))
                    ch(v.tensor_add(rcol[3][:, :], rcol[1][:, 0:1], rcol[2][:, 0:1]))
                    ch(v.reciprocal(rcol[4][:, :], rcol[3][:, 0:1]))
                    ch(v.tensor_scalar(route_tm[:, :], rt_t[:, :], rcol[4][:, 0:1], None, ALU.mult))
                    ch(v.tensor_copy(rtpad[:, 0:8], route_tm[:, :]))
            with nc.Block() as b:
                @b.tensor
                def _(t, tt=tt):
                    t.matmul(prt[:, :], rtpad[:, :], cm["identb"][:, :], start=True, stop=True)
            with nc.Block() as b:
                @b.scalar
                def _(s, tt=tt):
                    s.copy(routef[:, tt * 128:(tt + 1) * 128], prt[0:8, :])
        with nc.Block() as b:
            @b.vector
            def _(v):
                v.tensor_copy(rfb[:, :], routef[:, :])
        pph = pes.enter_context(nc.psum_tensor("pph", [128, 1024], F32))
        with nc.Block() as b:
            @b.tensor
            def _(t):
                for h in range(2):
                    t.matmul(pph[:, h * 512:(h + 1) * 512], eexpb[:, :],
                             rfb[0:8, h * 512:(h + 1) * 512], start=True, stop=True)
        with nc.Block() as b:
            @b.vector
            def _(v):
                v.tensor_copy(wf1b[:, :], pph[:, :])
        with nc.Block() as b:
            @b.tensor
            def _(t):
                for kc in range(32):
                    for h in range(2):
                        t.matmul(pph[:, h * 512:(h + 1) * 512], a1sl[:, kc * 128:(kc + 1) * 128],
                                 xn1b[kc][:, h * 512:(h + 1) * 512], start=(kc == 0), stop=(kc == 31))
        with nc.Block() as b:
            @b.vector
            def _(v):
                v.tensor_mul(hw1b[:, :], pph[:, :], wf1b[:, :])

    fes = ExitStack()
    wsl = [ctx.sbx(fes, f"wslF{i}", [128, 4096], BF16) for i in range(2)]
    bfsl = [ctx.sbx(fes, f"bfslF{i}", [128, 128], BF16) for i in range(2)]
    x1buf = [ctx.sbx(fes, f"x1bufF{i}", [128, 1024], F32) for i in range(4)]
    x2buf = [ctx.sbx(fes, f"x2bufF{i}", [128, 1024], F32) for i in range(4)]
    x2fin = [ctx.sbx(fes, f"x2finF{i}", [128, 1024], F32) for i in range(4)]
    mambuf = [ctx.sbx(fes, f"mambufF{i}", [128, 1024], F32) for i in range(4)]
    slm = {}
    with nc.Block() as b:
        @b.gpsimd
        def _(gp):
            slm[0] = dW.dinc(gp.dma_start(wsl[0][:, :], slab_ap(wb1_l, 0, 1, 4096)))
            slm[(1, 0)] = dW.dinc(gp.dma_start(bfsl[0][:, :], slab_ap(b1f_l, 0, 1, 128)))

        @b.sync
        def _(sy):
            xmark[100] = dA.dinc(sy.dma_start(x1buf[0][:, :], x1d[0:128, :]))
            xmark[200] = dA.dinc(sy.dma_start(mambuf[0][:, :], mamd[0:128, :]))
    with ExitStack() as pesF:
        pp = [pesF.enter_context(nc.psum_tensor(f"ppF{i}", [128, 1024], F32)) for i in range(2)]
        for i in range(35):
            with nc.Block() as b:
                @b.gpsimd
                def _(gp, i=i):
                    if i + 1 < 32:
                        slm[i + 1] = dW.dinc(gp.dma_start(
                            wsl[(i + 1) % 2][:, :], slab_ap(wb1_l, (i + 1) * 128, 1, 4096)))
                        slm[(1, i + 1)] = dW.dinc(gp.dma_start(
                            bfsl[(i + 1) % 2][:, :], slab_ap(b1f_l, (i + 1) * 128, 1, 128)))

                @b.sync
                def _(sy, i=i):
                    if i + 1 < 32:
                        xmark[100 + i + 1] = dA.dinc(sy.dma_start(
                            x1buf[(i + 1) % 4][:, :], x1d[(i + 1) * 128:(i + 2) * 128, :]))
                        xmark[200 + i + 1] = dA.dinc(sy.dma_start(
                            mambuf[(i + 1) % 4][:, :], mamd[(i + 1) * 128:(i + 2) * 128, :]))
                    if 3 <= i < 35:
                        j = i - 3
                        dO.dinc(sy.dma_start(outT[j * 128:(j + 1) * 128, :], x2fin[j % 4][:, :]))

                if i < 32:
                    @b.tensor
                    def _(t, i=i):
                        t.wait_ge(dW.h, slm[i])
                        t.wait_ge(dW.h, slm[(1, i)])
                        for kc in range(32):
                            for h in range(2):
                                hs = slice(h * 512, (h + 1) * 512)
                                t.matmul(pp[i % 2][:, hs],
                                         wsl[i % 2][:, kc * 128:(kc + 1) * 128],
                                         xn1b[kc][:, hs], start=(kc == 0), stop=False)
                        for h in range(2):
                            hs = slice(h * 512, (h + 1) * 512)
                            t.matmul(pp[i % 2][:, hs], bfsl[i % 2][:, :], hw1b[:, hs],
                                     start=False, stop=True)
                if 1 <= i <= 32:
                    @b.vector
                    def _(v, i=i):
                        j = i - 1
                        v.wait_ge(dA.h, xmark[100 + j])
                        for h in range(2):
                            hs = slice(h * 512, (h + 1) * 512)
                            v.scalar_tensor_tensor(x2buf[j % 4][:, hs], pp[j % 2][:, hs],
                                                   bb1[:, j:j + 1], x1buf[j % 4][:, hs],
                                                   ALU.add, ALU.add)
                if 2 <= i <= 33:
                    @b.gpsimd
                    def _(gp, i=i):
                        j = i - 2
                        chg = EChain(gp, ctx.sem("chG"))
                        gp.wait_ge(dA.h, xmark[200 + j])
                        chg(gp.tensor_add(x2fin[j % 4][:, :], x2buf[j % 4][:, :], mambuf[j % 4][:, :]))
                        chg(gp.tensor_add(x2fin[j % 4][:, :], x2fin[j % 4][:, :], xn1b[j][:, :]))
    with nc.Block() as b:
        @b.sync
        def _(sy):
            sy.wait_ge(dA.h, dA.v)
            sy.wait_ge(dO.h, dO.v)

        @b.gpsimd
        def _(gp):
            gp.wait_ge(dW.h, dW.v)
    fes.close()
    ctx.end_group()
    return ctx


_CACHE = {}


def _run_group(name, builder, arrays):
    if name not in _CACHE:
        _CACHE[name] = builder()
    ctx = _CACHE[name]
    in_maps = []
    for r in range(8):
        m = {}
        for n in ctx.in_names:
            a = arrays[n]
            m[n] = a[r] if isinstance(a, list) else a
        in_maps.append(m)
    res = run_bass_kernel_spmd(ctx.nc, in_maps, core_ids=list(range(8)))
    for n in ctx.out_names:
        arrays[n] = [np.asarray(res.results[r][n]) for r in range(8)]
    return res


GROUP_BUILDERS = [("g1", build_g1), ("g2", build_g2), ("g3", build_g3), ("g4", build_g4), ("g5", build_g5), ("g6", build_g6)]

FUSED = True


def build_full():
    ctx = Ctx("full", fused=True)
    for _, bld in GROUP_BUILDERS:
        bld(ctx)
    return ctx


def run_all(inputs, upto=None):
    shared, per = stage_inputs(inputs)
    arrays = dict(shared)
    for k in per[0]:
        arrays[k] = [p[k] for p in per]
    if FUSED and upto is None:
        _run_group("full", build_full, arrays)
        return arrays
    for name, builder in GROUP_BUILDERS:
        _run_group(name, builder, arrays)
        if upto is not None and name == upto:
            break
    return arrays


def kernel(**inputs):
    arrays = run_all(inputs)
    out = np.stack([np.asarray(arrays["outT"][r]).T for r in range(8)], axis=0)
    return out.reshape(2, 4096, 4096)
